# version 05
# baseline (speedup 1.0000x reference)
import contextlib
import os
import numpy as np
import concourse.bass as bass
from concourse.bass import IndirectOffsetOnAxis as IOA
import concourse.mybir as mybir

F32 = mybir.dt.float32
BF16 = mybir.dt.bfloat16
U32 = mybir.dt.uint32
I32 = mybir.dt.int32
I16 = mybir.dt.int16
AF = mybir.ActivationFunctionType
ALU = mybir.AluOpType
AX = mybir.AxisListType

D = 1024
HV, HK, DK, DV = 16, 8, 64, 64
KDIM, VDIM = HK * DK, HV * DV
CONVD = 2 * KDIM + VDIM
KS = 4
CC = 64
G = 512
E, CAP = 16, 256
VOC = 51200
EPS = 1e-6
LNEPS = 1e-5
DFF = 4096

ENGS = ("pe", "act", "dve", "pool", "sp")


class KB:
    def __init__(self, T=8192):
        self.T = T
        self.NG = T // G
        self.CH_G = G // CC
        self.NTT = G // 128
        self.NCH = T // CC
        self.nc = bass.Bass("TRN2", target_bir_lowering=False)
        self.PBLIM = int(os.environ.get("PBLIM", "99"))
        self.PCLIM = int(os.environ.get("PCLIM", "99"))
        self.items = {e: [] for e in ENGS}
        self.cnt = {}
        self.watermark = {e: {} for e in ENGS}
        self.sems = {}
        self.stack = contextlib.ExitStack()
        self.scope = None
        self.dram = {}
        self.bank_free = {}

    def wait(self, eng, tokens):
        agg = {}
        for sem, val in tokens:
            if val > agg.get(sem, 0):
                agg[sem] = val
        for sem, val in agg.items():
            if val <= 0:
                continue
            if self.watermark[eng].get(sem, 0) >= val:
                continue
            self.items[eng].append(("wait", sem, val))
            self.watermark[eng][sem] = val

    def emit(self, eng, fn, reads=(), incs=()):
        for sem, _ in incs:
            if sem not in self.sems:
                self.sems[sem] = self.stack.enter_context(self.nc.semaphore(sem))
        self.wait(eng, reads)
        self.items[eng].append(("op", fn, list(incs)))
        out = []
        for sem, amt in incs:
            self.cnt[sem] = self.cnt.get(sem, 0) + amt
            out.append((sem, self.cnt[sem]))
        return out

    def step(self, eng, fn, reads=()):
        return self.emit(eng, fn, reads=reads, incs=[(eng, 1)])

    def dma(self, q, out_ap, in_ap, reads=(), sem=None, **kw):
        if sem is None:
            sem = {"sp": "dx", "act": "dact", "pool": "dpool"}[q]
        if sem not in self.sems:
            self.sems[sem] = self.stack.enter_context(self.nc.semaphore(sem))
        return self.emit(q, lambda e: e.dma_start(out=out_ap, in_=in_ap, **kw),
                         reads=reads, incs=[(sem, 16)])

    def tok1(self, sem):
        return (sem, self.cnt.get(sem, 0))

    def barrier(self):
        toks = [self.tok1(x) for x in ("pe", "act", "dve", "pool")]
        toks += [self.tok1(x) for x in self.cnt if x.startswith("d")]
        for e in ENGS:
            self.wait(e, toks)

    def din(self, name, shape, dt=F32):
        ap = self.nc.dram_tensor(name, shape, dt, kind="ExternalInput").ap()
        self.dram[name] = ap
        return ap

    def dout(self, name, shape, dt=F32):
        ap = self.nc.dram_tensor(name, shape, dt, kind="ExternalOutput").ap()
        self.dram[name] = ap
        return ap

    def dint(self, name, shape, dt=F32):
        kind = "Internal"
        if os.environ.get("DBGFIN") and name in ("fin_h", "aff_h"):
            kind = "ExternalOutput"
        ap = self.nc.dram_tensor(name, shape, dt, kind=kind).ap()
        self.dram[name] = ap
        return ap

    def sbuf(self, name, shape, dt=F32, scope=None):
        st = scope if scope is not None else (self.scope or self.stack)
        return st.enter_context(self.nc.sbuf_tensor(name + "_sb", shape, dt))

    def psum(self, name, shape, dt=F32, scope=None):
        st = scope if scope is not None else (self.scope or self.stack)
        return st.enter_context(self.nc.psum_tensor(name, shape, dt))

    def declare_common(self):
        T, NG = self.T, self.NG
        nc = self.nc
        for n in ["pe", "act", "dve", "pool", "spp", "dx", "dact", "dpool"]:
            self.sems[n] = self.stack.enter_context(nc.semaphore(n))

        self.x_d = self.din("x", [T, D])
        self.wqkv_d = self.din("wqkv", [D, CONVD])
        self.wz_d = self.din("wz", [D, VDIM])
        self.wab_d = self.din("wab", [D, 64])
        self.dtb_d = self.din("dtb", [16, 1])
        self.alog_d = self.din("alog", [16, 1])
        self.convdiag_d = self.din("convdiag", [KS, 16, 128, 128])
        self.normw_d = self.din("normw", [1, DV])
        self.wout_d = self.din("wout", [VDIM, D])
        self.gqt_d = self.din("gqt", [D, E])
        self.ltemp_d = self.din("ltemp", [1, 1])
        self.w1_d = self.din("w1", [E, D, DFF])
        self.b1_d = self.din("b1", [E, 128, DFF // 128])
        self.w2_d = self.din("w2", [E, DFF, D])
        self.b2_d = self.din("b2", [E, D])
        self.id128b_d = self.din("id128b", [128, 128], BF16)
        self.id128f_d = self.din("id128f", [128, 128])
        self.maskA_d = self.din("maskA", [CC, 8 * CC], BF16)
        self.maskM_d = self.din("maskM", [CC, 8 * CC], BF16)
        self.ones3_d = self.din("ones3", [6, CC], BF16)
        self.neg3_d = self.din("neg3", [6, CC], BF16)
        self.blkh64_d = self.din("blkh64", [128, 2], BF16)
        self.blk16_d = self.din("blk16", [128, 8])
        self.exp16_d = self.din("exp16", [8, 128])
        self.comb16_d = self.din("comb16", [128, 16])
        self.expc16_d = self.din("expc16", [16, 128])
        self.cvals_d = self.din("cvals", [1, 4])
        self.out_d = self.dout("out", [T, D])

        self.qkvT_h = self.dint("qkvT_h", [NG, 128, 16, G], BF16)
        self.zs_h = self.dint("zs_h", [NG, 128, self.NTT, D], BF16)
        self.bg_h = self.dint("bg_h", [16, 2, T])
        self.sumk_h = self.dint("sumk_h", [16, 2, T])
        self.xln2_h = self.dint("xln2_h", [T, D], BF16)
        self.logit_h = self.dint("logit_h", [16, T])
        self.tk_h = self.dint("tk_h", [2, 128, 32], U32)

        sb = self.sbuf
        cm = self.stack
        self.id128b = sb("id128b", [128, 128], BF16, cm)
        self.id128f = sb("id128f", [128, 128], F32, cm)
        self.maskA = sb("maskA", [CC, 8 * CC], BF16, cm)
        self.maskM = sb("maskM", [CC, 8 * CC], BF16, cm)
        self.ones3 = sb("ones3", [6, CC], BF16, cm)
        self.neg3 = sb("neg3", [6, CC], BF16, cm)
        self.blkh64 = sb("blkh64", [128, 2], BF16, cm)
        self.blk16 = sb("blk16", [128, 8], F32, cm)
        self.exp16 = sb("exp16", [8, 128], F32, cm)
        self.comb16 = sb("comb16", [128, 16], F32, cm)
        self.expc16 = sb("expc16", [16, 128], F32, cm)
        self.normw_r = sb("normw_r", [128, DV], F32, cm)
        self.dtb = sb("dtb", [16, 1], F32, cm)
        self.nega = sb("nega", [16, 1], F32, cm)
        self.alog_s = sb("alog_s", [16, 1], F32, cm)
        self.ltemp = sb("ltemp", [128, 1], F32, cm)
        self.cvals = sb("cvals", [128, 4], F32, cm)



        d = self.dram
        toks = []
        for t, ap in [(self.id128b, d["id128b"][:]), (self.id128f, d["id128f"][:]),
                      (self.maskA, d["maskA"][:]),
                      (self.maskM, d["maskM"][:]), (self.ones3, d["ones3"][:]),
                      (self.neg3, d["neg3"][:]), (self.blkh64, d["blkh64"][:]),
                      (self.blk16, d["blk16"][:]), (self.exp16, d["exp16"][:]),
                      (self.comb16, d["comb16"][:]), (self.expc16, d["expc16"][:]),
                      (self.dtb, d["dtb"][:]), (self.alog_s, d["alog"][:])]:
            toks += self.dma("sp", t[:], ap)
        toks += self.dma("sp", self.normw_r[:], d["normw"][0:1, :].partition_broadcast(128))
        toks += self.dma("sp", self.ltemp[:], d["ltemp"][0:1, :].partition_broadcast(128))
        toks += self.dma("sp", self.cvals[:], d["cvals"][0:1, :].partition_broadcast(128))
        for i, v in enumerate([LNEPS, 1.0, EPS, 0.5]):
            self.nc.const_aps.register(F32, v, self.cvals[:, i:i + 1])
        self.CONST_T = toks[-1]
        t1 = self.step("act", lambda e: e.activation(self.nega[:], self.alog_s[:], AF.Exp),
                       reads=[self.CONST_T])
        self.nega_t = self.step("dve", lambda e: e.tensor_scalar_mul(
            self.nega[:], self.nega[:], -1.0), reads=t1)

    def pass_a(self):
        T, NG, NTT = self.T, self.NG, self.NTT
        self.scope = contextlib.ExitStack()
        sb = self.sbuf
        PS = [self.psum(f"ps{i}", [128, 512]) for i in range(8)]
        self.PSb = [p[:].bitcast(BF16) for p in PS]
        wqkv = sb("wqkv", [128, 8, CONVD], BF16)
        wz = sb("wz", [128, 8, VDIM], BF16)
        wab = sb("wab", [128, 8, 64], BF16)
        convdiag = sb("convdiag", [128, KS * 16, 128], BF16)
        xg = sb("xg", [128, 2, NTT, D])
        xln = sb("xln", [128, NTT, D], BF16)
        xlnT = sb("xlnT", [128, 2, 8, G], BF16)
        mixedT = sb("mixedT", [128, 16, KS - 1 + G], BF16)
        qkvTo = sb("qkvTo", [128, 16, G], BF16)
        zso = sb("zso", [128, NTT, D], BF16)
        lnstat = sb("lnstat", [128, 2, NTT])
        mu = sb("mu", [128, NTT])
        rsq = sb("rsq", [128, NTT])
        scrap = sb("scrap", [128, D])
        scrapq = sb("scrapq", [128, G], BF16)
        scrap2 = sb("scrap2", [16, G])
        bgs = sb("bgs", [16, 2, G])
        sumks = sb("sumks", [2, 8, G])
        psba = PS[6]
        pssq = PS[7]

        d = self.dram
        wt = []
        wt += self.dma("pool", wqkv[:], d["wqkv"][:].rearrange("(c p) f -> p c f", p=128))
        wt += self.dma("pool", wz[:], d["wz"][:].rearrange("(c p) f -> p c f", p=128))
        wt += self.dma("pool", wab[:], d["wab"][:].rearrange("(c p) f -> p c f", p=128))
        wt += self.dma("pool", convdiag[:], d["convdiag"][:].rearrange("k c p q -> p (k c) q"))
        W_T = wt[-1]

        self.qkvT_done = [None] * NG
        self.zs_done = [None] * NG
        self.bg_done = [None] * NG
        self.sumk_done = [None] * NG
        xg_free = [[], []]
        xlnT_free = [[], []]
        xln_free = []
        qkvTo_free = []
        zso_free = []
        bgs_free = []
        sumks_free = []

        mixT_free = self.step("dve", lambda e: e.memset(
            mixedT[:, :, 0:KS - 1], 0.0))
        for g in range(NG):
            r = g % 2
            xg_T = self.dma("sp", xg[:, r], self.x_d[g * G:(g + 1) * G, :]
                            .rearrange("(i p) d -> p i d", p=128), reads=xg_free[r],
                            sem=f"dxg{r}")
            ln_toks = []
            for i in range(NTT):
                ln_toks += self.step("dve", lambda e, i=i, r=r: e.tensor_reduce(
                    lnstat[:, 0, i:i + 1], xg[:, r, i, :], AX.X, ALU.add),
                    reads=xg_T + ([self.tok1("dve")] if i == 0 else []))
                ln_toks += self.step("act", lambda e, i=i, r=r: e.activation(
                    scrap[:], xg[:, r, i, :], AF.Square,
                    accum_out=lnstat[:, 1, i:i + 1]),
                    reads=xg_T + [self.tok1("act")])
            t1 = self.step("dve", lambda e: e.tensor_scalar_mul(
                mu[:], lnstat[:, 0, :], 1.0 / D), reads=ln_toks)
            t2 = self.step("dve", lambda e: e.scalar_tensor_tensor(
                rsq[:], mu[:], -1.0, mu[:], ALU.mult, ALU.mult), reads=t1)
            t3 = self.step("dve", lambda e: e.scalar_tensor_tensor(
                rsq[:], lnstat[:, 1, :], 1.0 / D, rsq[:], ALU.mult, ALU.add),
                reads=t2 + ln_toks)
            t4 = self.step("act", lambda e: e.activation(
                rsq[:], rsq[:], AF.Ln, bias=LNEPS), reads=t3)
            t5 = self.step("act", lambda e: e.activation(
                rsq[:], rsq[:], AF.Exp, scale=-0.5), reads=t4)
            xln_toks = []
            for i in range(NTT):
                xln_toks += self.step("dve", lambda e, i=i, r=r: e.tensor_scalar(
                    xln[:, i, :], xg[:, r, i, :], mu[:, i:i + 1], rsq[:, i:i + 1],
                    ALU.subtract, ALU.mult),
                    reads=t5 + (xln_free if i == 0 else []))
            xg_free[r] = xln_toks[-1:]
            tr_done = []
            for i in range(NTT):
                for dcq in range(2):
                    bank = PS[dcq]
                    key = ("psA", dcq)
                    mm = []
                    for k in range(4):
                        dc = dcq * 4 + k
                        bankb = self.PSb[dcq]
                        mm = self.emit("pe", lambda e, i=i, dc=dc, k=k, bankb=bankb:
                                       e.transpose(bankb[:, k * 128:(k + 1) * 128],
                                                   xln[:, i, dc * 128:(dc + 1) * 128],
                                                   self.id128b[:]),
                                       reads=(xln_toks if k == 0 else []) +
                                             list(self.bank_free.get(key, ())) +
                                             ([self.CONST_T] if g == 0 else []),
                                       incs=[("pe", 1)] if k == 3 else [])
                    ev = self.step("act", lambda e, i=i, r=r, dcq=dcq, bankb=self.PSb[dcq]: e.copy(
                        xlnT[:, r, dcq * 4:dcq * 4 + 4, i * 128:(i + 1) * 128],
                        bankb[:, 0:512].rearrange("p (a b) -> p a b", b=128)),
                        reads=mm + (xlnT_free[r] if (i == 0 and dcq == 0) else []))
                    self.bank_free[key] = ev
                    tr_done += ev
            xln_free = tr_done[-1:]
            mixed_done = []
            for fc in range(16):
                bank = PS[2 + (fc % 2)]
                key = ("psB", fc % 2)
                mm = []
                for dc in range(8):
                    mm = self.emit("pe", lambda e, fc=fc, dc=dc, r=r, bank=bank:
                                   e.matmul(bank[:, 0:512],
                                            wqkv[:, dc, fc * 128:(fc + 1) * 128],
                                            xlnT[:, r, dc, :],
                                            start=(dc == 0), stop=(dc == 7)),
                                   reads=(tr_done if dc == 0 else []) +
                                         list(self.bank_free.get(key, ())) +
                                         ([W_T] if g == 0 and fc == 0 else []),
                                   incs=[("pe", 1)] if dc == 7 else [])
                ev = self.step("dve", lambda e, fc=fc, bank=bank: e.tensor_copy(
                    mixedT[:, fc, KS - 1:], bank[:, 0:512]),
                    reads=mm + (mixT_free if fc == 0 else []))
                self.bank_free[key] = ev
                mixed_done += ev
            conv_done = []
            for fc in range(16):
                bank = PS[4 + (fc % 2)]
                key = ("psC", fc % 2)
                mm = []
                for k in range(KS):
                    mm = self.emit("pe", lambda e, fc=fc, k=k, bank=bank:
                                   e.matmul(bank[:, 0:512],
                                            convdiag[:, k * 16 + fc, :],
                                            mixedT[:, fc, k:k + G],
                                            start=(k == 0), stop=(k == KS - 1)),
                                   reads=(mixed_done if k == 0 else []) +
                                         list(self.bank_free.get(key, ())),
                                   incs=[("pe", 1)] if k == KS - 1 else [])
                ev = self.step("act", lambda e, fc=fc, bank=bank: e.activation(
                    qkvTo[:, fc, :], bank[:, 0:512], AF.Silu),
                    reads=mm + (qkvTo_free if fc == 0 else []))
                self.bank_free[key] = ev
                conv_done += ev
            padt = self.step("dve", lambda e: e.tensor_copy(
                mixedT[:, :, 0:KS - 1],
                mixedT[:, :, G:G + KS - 1]),
                reads=mixed_done + conv_done)
            mixT_free = padt + conv_done[-1:]
            st = self.dma("pool", self.qkvT_h[g], qkvTo[:], reads=conv_done, sem="dsq")
            self.qkvT_done[g] = st
            sq_last = []
            prev_mmq = []
            for fc in range(8):
                sq1 = self.step("act", lambda e, fc=fc: e.activation(
                    scrapq[:], qkvTo[:, fc, :], AF.Square),
                    reads=conv_done + [self.tok1("act")] + prev_mmq)
                mmq = self.step("pe", lambda e: e.matmul(
                    pssq[0:2, :], self.blkh64[:], scrapq[:], start=True, stop=True),
                    reads=sq1 + list(self.bank_free.get("pssq", ())) +
                          list(self.bank_free.get(("psD", 1), ())))
                prev_mmq = mmq
                ev = self.step("dve", lambda e, fc=fc: e.tensor_copy(
                    sumks[:, fc, :], pssq[0:2, :]),
                    reads=mmq + (sumks_free if fc == 0 else []))
                self.bank_free["pssq"] = ev
                sq_last = ev
            std = []
            for dup in range(2):
                for j in range(2):
                    dstk = self.sumk_h[:, :, g * G:(g + 1) * G] \
                        .rearrange("(k dup) j t -> k dup j t", dup=2)[:, dup, j, :] \
                        .rearrange("(a p) t -> p a t", a=4)
                    srck = sumks[:, j * 4:(j + 1) * 4, :]
                    std += self.dma("pool", dstk, srck, reads=sq_last,
                                    sem=f"dsk{dup}{j}")
            self.sumk_done[g] = std
            sumks_free = std
            qkvTo_free = st + sq_last
            z_done = []
            for i in range(NTT):
                for nh in range(2):
                    bank = PS[6 + nh]
                    key = ("psD", nh)
                    mm = []
                    for dc in range(8):
                        mm = self.emit("pe", lambda e, i=i, nh=nh, dc=dc, r=r, bank=bank:
                                       e.matmul(bank[:],
                                                xlnT[:, r, dc, i * 128:(i + 1) * 128],
                                                wz[:, dc, nh * 512:(nh + 1) * 512],
                                                start=(dc == 0), stop=(dc == 7)),
                                       reads=(tr_done if dc == 0 else []) +
                                             list(self.bank_free.get(key, ())) +
                                             (list(self.bank_free.get("psba" if nh == 0 else "pssq", ())) if dc == 0 else []),
                                       incs=[("pe", 1)] if dc == 7 else [])
                    ev = self.step("act", lambda e, i=i, nh=nh, bank=bank: e.activation(
                        zso[:, i, nh * 512:(nh + 1) * 512], bank[:], AF.Silu),
                        reads=mm + (zso_free if (i == 0 and nh == 0) else []))
                    self.bank_free[key] = ev
                    z_done += ev
            st2 = self.dma("pool", self.zs_h[g], zso[:], reads=z_done, sem="dsz")
            self.zs_done[g] = st2
            zso_free = st2
            mm = []
            for dc in range(8):
                mm = self.emit("pe", lambda e, dc=dc, r=r: e.matmul(
                    psba[0:64, :], wab[:, dc, :], xlnT[:, r, dc, :],
                    start=(dc == 0), stop=(dc == 7)),
                    reads=(tr_done if dc == 0 else []) +
                          list(self.bank_free.get("psba", ())) +
                          (list(self.bank_free.get(("psD", 0), ())) if dc == 0 else []),
                    incs=[("pe", 1)] if dc == 7 else [])
            xlnT_free[r] = mm[-1:]
            tb1 = self.step("act", lambda e: e.activation(
                bgs[:, 0, :], psba[0:16, :], AF.Exp, scale=-1.0),
                reads=mm + list(bgs_free))
            tb2 = self.step("dve", lambda e: e.tensor_scalar_add(
                bgs[:, 0, :], bgs[:, 0, :], 1.0), reads=tb1)
            tb3 = self.step("dve", lambda e: e.reciprocal(
                bgs[:, 0, :], bgs[:, 0, :]), reads=tb2)
            ta1 = self.step("act", lambda e: e.activation(
                bgs[:, 1, :], psba[32:48, :], AF.Abs, bias=self.dtb[:]), reads=mm)
            ta2 = self.step("act", lambda e: e.activation(
                bgs[:, 1, :], bgs[:, 1, :], AF.Exp, scale=-1.0), reads=ta1)
            ta3 = self.step("act", lambda e: e.activation(
                bgs[:, 1, :], bgs[:, 1, :], AF.Ln, bias=1.0), reads=ta2)
            ta4 = self.step("act", lambda e: e.activation(
                scrap2[:], psba[32:48, :], AF.Relu, bias=self.dtb[:]),
                reads=mm + ta3 + [self.tok1("act")])
            ta5 = self.step("dve", lambda e: e.tensor_add(
                bgs[:, 1, :], bgs[:, 1, :], scrap2[:]), reads=ta4 + ta3)
            ta6 = self.step("dve", lambda e: e.tensor_scalar_mul(
                bgs[:, 1, :], bgs[:, 1, :], self.nega[:, 0:1]),
                reads=ta5 + self.nega_t)
            self.bank_free["psba"] = ta4 + tb1
            stb = self.dma("pool", self.bg_h[:, :, g * G:(g + 1) * G], bgs[:],
                           reads=tb3 + ta6, sem="dsb")
            self.bg_done[g] = stb
            bgs_free = stb
        self.barrier()
        self.scope.close()
        self.scope = None

    def pass_b(self, dbg=False):
        T, NG, NTT, CH_G = self.T, self.NG, self.NTT, self.CH_G
        self.scope = contextlib.ExitStack()
        sb = self.sbuf
        d = self.dram

        wout = sb("wout", [128, 8, D], BF16)
        gqt = sb("gqt", [128, 8, E], BF16)
        qkvB = sb("qkvB", [128, 2, 16, G], BF16)
        zB = sb("zB", [128, NTT, D], BF16)
        xg2 = sb("xg2", [128, NTT, D])
        bgB = sb("bgB", [16, 2, G])
        lnkq = sb("lnkq", [16, 2, G])
        ccum = sb("ccum", [16, G])
        lnb = sb("lnb", [16, G])
        bhat = sb("bhat", [16, G])
        spl = sb("spl", [16, 9, G], BF16)
        zeros16 = sb("zeros16", [16, CC])
        colsrc = sb("colsrc", [16, 4, G])
        colT = sb("colT", [64, 2, 4, 16])
        stgB = sb("stgB", [3, 2048], BF16)
        stgM = sb("stgM", [3, 2048], BF16)
        stgA = sb("stgA", [3, 2048], BF16)
        onesc3 = sb("onesc3", [3, CC], BF16)
        negc3 = sb("negc3", [3, CC], BF16)
        expT = sb("expT", [64, 1024])
        NTt = sb("NTt", [64, 16, CC], BF16)
        Nn = sb("Nn", [64, 16, CC], BF16)
        ATt = sb("ATt", [64, 16, CC], BF16)
        PW = {p: sb(f"PW{p}", [64, 16, CC], BF16)
              for p in ["2T", "2n", "4T", "4n", "8T", "8n", "16T", "16n", "32T"]}
        X = sb("X", [64, 2, 16, 128], BF16)
        tokK = sb("tokK", [64, 4, 128], BF16)
        Ssb = sb("Ssb", [64, 2, 16, DV])
        Sbf = sb("Sbf", [128, 2, 16, DV], BF16)
        St = sb("St", [64, 16, DV])
        kw = sb("kw", [64, 16, CC], BF16)
        negWT = sb("negWT", [64, 16, CC], BF16)
        negWKT = sb("negWKT", [64, 16, CC], BF16)
        Uc = sb("Uc", [64, 16, CC], BF16)
        o2sb = sb("o2sb", [64, 16, DV], BF16)
        osb = sb("osb", [128, 16, DV])
        gated = sb("gated", [128, NTT, D], BF16)
        gatedT = sb("gatedT", [128, 8, G], BF16)
        lgs = sb("lgs", [16, G])
        lnstat = sb("lnstatB", [128, 2, NTT])
        mu = sb("muB", [128, NTT])
        rsq = sb("rsqB", [128, NTT])
        scrap = osb[:].rearrange("p h t -> p (h t)")
        ovar = sb("ovar", [128, 16, 1])

        pA, pB_, pC, pD, pE = (self.psum("pA", [128, 512]), self.psum("pB", [128, 512]),
                               self.psum("pC", [128, 1024]), self.psum("pD", [128, 1024]),
                               self.psum("pE", [128, 1024]))
        pCb = pC[:].bitcast(BF16)
        pDb = pD[:].bitcast(BF16)

        splits_h = self.dint("splits_h", [16, 9, T], BF16)

        wl = []
        wl += self.dma("pool", wout[:], d["wout"][:].rearrange("(c p) f -> p c f", p=128), sem="dwB")
        wl += self.dma("pool", gqt[:], d["gqt"][:].rearrange("(c p) f -> p c f", p=128), sem="dwB")
        W_T = wl[-1]
        z16 = self.step("dve", lambda e: e.memset(zeros16[:], 0.0))
        oc3 = self.step("dve", lambda e: e.memset(onesc3[:], 1.0))
        nc3 = self.step("dve", lambda e: e.memset(negc3[:], -1.0))
        sinit = self.step("dve", lambda e: e.memset(Ssb[:], 0.0))
        sinit += self.step("dve", lambda e: e.memset(Sbf[:], 0.0))
        srep = {0: sinit, 1: sinit}
        eitemp = sb("eitemp", [128, 1])
        eit = self.step("act", lambda e: e.activation(
            eitemp[:], self.ltemp[:], AF.Exp, scale=-1.0), reads=[self.CONST_T])

        bank = {k: [] for k in ["pA", "pB", "pC", "pD", "pE"]}
        ring_free = {}
        schain = sinit
        self.res_done = []

        for g in range(NG):
            r = g % 2
            gsl = slice(g * G, (g + 1) * G)
            lq = self.dma("sp", qkvB[:, r], self.qkvT_h[g], sem=f"dqB{r}",
                          reads=self.qkvT_done[g] + ring_free.get(("qkvB", r), []))
            lz = self.dma("sp", zB[:], self.zs_h[g], sem="dzB",
                          reads=self.zs_done[g] + ring_free.get(("zB",), []))
            lx = self.dma("sp", xg2[:], self.x_d[gsl, :]
                          .rearrange("(i p) d -> p i d", p=128), sem="dxB",
                          reads=ring_free.get(("xg2",), []))
            lbg = self.dma("act", bgB[:], self.bg_h[:, :, gsl], sem="dbB",
                           reads=self.bg_done[g] + ring_free.get(("bgB",), []))
            lkq = self.dma("act", lnkq[:], d["sumk_h"][:, :, gsl], sem="dkB",
                           reads=self.sumk_done[g] + ring_free.get(("lnkq",), []))
            lnt = self.step("act", lambda e: e.activation(
                lnkq[:], lnkq[:], AF.Ln, bias=EPS), reads=lkq)
            prevc = []
            v3 = lambda t: t.rearrange("h (c t) -> h c t", t=CC)
            srcs = [bgB[:, 1, :], ccum[:], scrap[0:16, 0:G]]
            cur = 0
            for si, s in enumerate([1, 2, 4, 8, 16, 32]):
                sv, dv = v3(srcs[cur]), v3(srcs[2 - (si % 2)])
                t1 = self.step("dve", lambda e, sv=sv, dv=dv, s=s: e.tensor_tensor(
                    dv[:, :, s:], sv[:, :, s:], sv[:, :, 0:CC - s], ALU.add),
                    reads=lbg + prevc + ([self.tok1("dve")] if si == 0 else []))
                t2 = self.step("dve", lambda e, sv=sv, dv=dv, s=s: e.tensor_copy(
                    dv[:, :, 0:s], sv[:, :, 0:s]))
                prevc = t1 + t2
                cur = 2 - (si % 2)
            assert cur == 1
            cts = prevc
            lnbt = self.step("act", lambda e, r=r: e.activation(
                lnb[:], bgB[:, 0, :], AF.Ln), reads=lbg + [self.tok1("act")])
            am1 = self.step("dve", lambda e, r=r: e.scalar_tensor_tensor(
                colsrc[:, 0, :], lnkq[:, 1, :], -0.5, ccum[:], ALU.mult, ALU.add),
                reads=lnt + cts)
            aa1 = self.step("dve", lambda e, r=r: e.scalar_tensor_tensor(
                colsrc[:, 2, :], lnkq[:, 0, :], -0.5, ccum[:], ALU.mult, ALU.add),
                reads=lnt + cts)
            aa2 = self.step("dve", lambda e: e.tensor_scalar_add(
                colsrc[:, 2, :], colsrc[:, 2, :], -0.5 * float(np.log(DK))), reads=aa1)
            bh1 = self.step("dve", lambda e, r=r: e.scalar_tensor_tensor(
                bhat[:], lnkq[:, 1, :], 0.5, ccum[:], ALU.mult, ALU.add),
                reads=lnt + cts)
            bh2 = self.step("dve", lambda e: e.tensor_sub(
                bhat[:], bhat[:], lnb[:]), reads=bh1 + lnbt)
            def emit_split(qi, get):
                t1 = self.step("dve", lambda e: e.tensor_copy(spl[:, qi * 3 + 0, :], get()))
                t2 = self.step("dve", lambda e: e.tensor_sub(scrap[0:16, 0:G], get(),
                                                             spl[:, qi * 3 + 0, :]),
                               reads=t1 + [self.tok1("dve")])
                t3 = self.step("dve", lambda e: e.tensor_copy(spl[:, qi * 3 + 1, :],
                                                              scrap[0:16, 0:G]), reads=t2)
                t4 = self.step("dve", lambda e: e.tensor_sub(scrap[0:16, 0:G],
                                                             scrap[0:16, 0:G],
                                                             spl[:, qi * 3 + 1, :]), reads=t3)
                t5 = self.step("dve", lambda e: e.tensor_copy(spl[:, qi * 3 + 2, :],
                                                              scrap[0:16, 0:G]), reads=t4)
                return t5
            s1 = self.wait("dve", bh2) or emit_split(0, lambda: bhat[:])
            s2 = self.wait("dve", am1) or emit_split(1, lambda: colsrc[:, 0, :])
            s3 = self.wait("dve", aa2) or emit_split(2, lambda: colsrc[:, 2, :])
            spl_done = s1 + s2 + s3
            sst = self.dma("pool", splits_h[:, :, gsl], spl[:], sem="dspl",
                           reads=spl_done)
            cs0, cs2 = [], []
            csx = []
            for c in range(CH_G):
                csl = slice(c * CC, (c + 1) * CC)
                csx += self.step("dve", lambda e, c=c, csl=csl: e.tensor_scalar(
                    colsrc[:, 1, csl], bhat[:, csl],
                    ccum[:, c * CC + CC - 1:c * CC + CC], -1.0,
                    ALU.subtract, ALU.mult), reads=bh2 + cts)
                csx += self.step("dve", lambda e, c=c, csl=csl: e.tensor_scalar(
                    colsrc[:, 3, csl], ccum[:, csl],
                    0.0, ccum[:, c * CC + CC - 1:c * CC + CC],
                    ALU.mult, ALU.add), reads=cts + [self.tok1("dve")])
            cse = self.step("act", lambda e: e.activation(
                colsrc[:], colsrc[:], AF.Exp), reads=csx + am1 + aa2 + s2 + s3 + [self.tok1("act")])

            for c in range(CH_G):
                if self.PBLIM <= 1:
                    continue
                gc = g * CH_G + c
                sp_old, sp_new = gc % 2, (gc + 1) % 2
                rr = c % 2
                csl = slice(c * CC, (c + 1) * CC)
                if c % 2 == 0:
                    c2 = slice(c * CC, (c + 2) * CC)
                    stg_toks = []
                    for stg, q0 in [(stgB, 0), (stgM, 3), (stgA, 6)]:
                        srcap = d["splits_h"][:, q0:q0 + 3, g * G + c * CC: g * G + (c + 2) * CC]
                        srcap = srcap.rearrange("h p t -> p h t")
                        stg_toks += self.dma("act", stg[:].rearrange("p (h t) -> p h t", t=128),
                                             srcap, sem="dstg",
                                             reads=sst + ring_free.get(("stg",), []))
                    self._stg_toks = stg_toks
                srr = (c // 2) % 2
                ctk = []
                for q in range(4):
                    ctk = self.emit("pe", lambda e, q=q, csl=csl: e.transpose(
                        pA[0:64, q * 16:(q + 1) * 16],
                        colsrc[:, q, csl], self.id128f[0:16, 0:16]),
                        reads=(cse + bank["pA"] if q == 0 else []),
                        incs=[("pe", 1)] if q == 3 else [])
                cte = self.step("dve", lambda e, rr=rr: e.tensor_copy(
                    colT[:, rr].rearrange("p a b -> p (a b)"), pA[0:64, 0:64]),
                    reads=ctk + ring_free.get(("colT", rr), []))
                bank["pA"] = cte
                if self.PBLIM <= 2:
                    continue
                gr = []
                seen = {0: False, 1: False}
                for kh in range(8):
                    par = kh % 2
                    j = kh // 2
                    fq, fk = kh // 2, 4 + kh // 2
                    base = par * 64
                    bnk = pA if par == 0 else pB_
                    first = not seen[par]
                    seen[par] = True
                    rd0 = (lq + cte + bank["pA"] if par == 0 else lq + bank["pB"]) if first else []
                    gr += self.emit("pe", lambda e, j=j, base=base, fk=fk, r=r, csl=csl, bnk=bnk:
                                    e.matmul(bnk[0:64, j * 64:(j + 1) * 64],
                                             qkvB[base:base + 64, r, fk, csl],
                                             qkvB[base:base + 64, r, fk, csl],
                                             start=True, stop=True),
                                    reads=rd0,
                                    incs=[("pe", 1)])
                    gr += self.emit("pe", lambda e, j=j, base=base, fk=fk, fq=fq, r=r, csl=csl, bnk=bnk:
                                    e.matmul(bnk[0:64, 256 + j * 64:256 + (j + 1) * 64],
                                             qkvB[base:base + 64, r, fk, csl],
                                             qkvB[base:base + 64, r, fq, csl],
                                             start=True, stop=True),
                                    reads=[],
                                    incs=[("pe", 1)])
                if self.PBLIM <= 3:
                    continue
                tile_toks = {}
                for kind in range(2):
                    stg_a = stgM if kind == 0 else stgA
                    maskc = self.maskM if kind == 0 else self.maskA
                    am = []
                    for half in range(2):
                        am += self.emit("pe", lambda e, half=half, maskc=maskc:
                                        e.matmul(pC[0:64, half * 512:(half + 1) * 512],
                                                 self.id128b[0:64, 0:64],
                                                 maskc[:, 0:512],
                                                 start=True, stop=False),
                                        reads=(bank["pC"] + self._stg_toks if half == 0 and kind == 0 else []) +
                                              (bank["pC"] if half == 0 and kind == 1 else []),
                                        incs=[])
                    for h in range(16):
                        off = h * 128 + (c % 2) * 64
                        am += self.emit("pe", lambda e, h=h, off=off, stg_a=stg_a:
                                        e.matmul(pC[0:64, h * 64:(h + 1) * 64],
                                                 onesc3[:], stg_a[:, off:off + 64],
                                                 start=False, stop=False), incs=[])
                        am += self.emit("pe", lambda e, h=h, off=off:
                                        e.matmul(pC[0:64, h * 64:(h + 1) * 64],
                                                 stgB[:, off:off + 64], negc3[:],
                                                 start=False, stop=(h in (7, 15))),
                                        incs=[("pe", 1)] if h == 15 else [])
                    ex = self.step("act", lambda e: e.activation(
                        expT[:], pC[0:64, :], AF.Exp),
                        reads=am + [self.tok1("act")] +
                              (tile_toks[0] if kind == 1 else
                               ring_free.get(("expT",), [])))
                    bank["pC"] = ex
                    co = 0 if kind == 0 else 256
                    dst = NTt if kind == 0 else ATt
                    sc = -1.0 if kind == 0 else 1.0
                    mt = []
                    mt_bank = {0: [], 1: []}
                    for par in range(2):
                        gview = (pA if par == 0 else pB_)[0:64, co:co + 256] \
                            .rearrange("p (j t) -> p j t", t=64)
                        for u in range(2):
                            tk = self.step("dve", lambda e, dst=dst, sc=sc, gview=gview, par=par, u=u:
                                           e.scalar_tensor_tensor(
                                               dst[:].rearrange("p (j par u) t -> p j par u t", par=2, u=2)[:, :, par, u, :],
                                               expT[:].rearrange("p (j par u t) -> p j par u t", par=2, u=2, t=64)[:, :, par, u, :],
                                               sc, gview, ALU.mult, ALU.mult),
                                           reads=(ex + gr +
                                                  (ring_free.get(("NTt",), []) if kind == 0 else
                                                   ring_free.get(("ATt",), []))) if (par == 0 and u == 0) else [])
                            mt += tk
                            mt_bank[par] += tk
                    tile_toks[kind] = mt
                    if kind == 0:
                        mt_bank0 = mt_bank
                    else:
                        mt_bank1 = mt_bank
                bank["pA"] = mt_bank0[0][-1:] + mt_bank1[0][-1:]
                bank["pB"] = mt_bank0[1][-1:] + mt_bank1[1][-1:]
                ring_free[("expT",)] = tile_toks[1]
                ring_free[("NTt",)] = []
                ring_free[("ATt",)] = []
                if self.PBLIM <= 4:
                    continue
                mntr = []
                for h in range(16):
                    mntr = self.emit("pe", lambda e, h=h: e.transpose(
                        pDb[0:64, h * 64:(h + 1) * 64], NTt[:, h, :],
                        self.id128b[0:64, 0:64]),
                        reads=(tile_toks[0] + bank["pD"] if h == 0 else []),
                        incs=[("pe", 1)] if h == 15 else [])
                nne = self.step("act", lambda e: e.copy(
                    Nn[:].rearrange("p h t -> p (h t)"), pDb[0:64, 0:1024]), reads=mntr)
                bank["pD"] = nne
                tkt = []
                for j in range(8):
                    tkt = self.emit("pe", lambda e, j=j, r=r, csl=csl: e.transpose(
                        pDb[0:64, 1024 + j * 128:1024 + (j + 1) * 128] if False else
                        pDb[0:64, j * 128:(j + 1) * 128],
                        qkvB[:, r, 8 + j, csl], self.id128b[:]),
                        reads=(nne + bank["pD"] if j == 0 else []),
                        incs=[("pe", 1)] if j == 7 else [])
                xve = self.step("act", lambda e: e.copy(
                    X[:, 0, :, 0:64],
                    pDb[0:64, 0:1024].rearrange("p (h w) -> p h w", w=64)),
                    reads=tkt + ring_free.get(("X",), []))
                ring_free[("X",)] = []
                bank["pD"] = xve
                ktt = []
                for j in range(4):
                    ktt = self.emit("pe", lambda e, j=j, r=r, csl=csl: e.transpose(
                        pDb[0:64, j * 128:(j + 1) * 128],
                        qkvB[:, r, 4 + j, csl], self.id128b[:]),
                        reads=(xve if j == 0 else []),
                        incs=[("pe", 1)] if j == 3 else [])
                kte = self.step("act", lambda e: e.copy(
                    tokK[:].rearrange("p j w -> p (j w)"), pDb[0:64, 0:512]), reads=ktt)
                bank["pD"] = kte
                tokK3 = tokK[:].rearrange("p j w -> p (j w)") \
                    .rearrange("p (ab t) -> p ab t", t=64)
                x0k = []
                for u in range(2):
                    e1u = colT[:, rr, 0, :].rearrange("p (ab u) -> p ab u", u=2)[:, :, u] \
                        .unsqueeze(-1).to_broadcast([64, 8, 64])
                    x0k += self.step("dve", lambda e, u=u, e1u=e1u, tokK3=tokK3:
                                     e.tensor_tensor(
                                         X[:, 0, :, 64:128]
                                         .rearrange("p (ab u) w -> p ab u w", u=2)[:, :, u, :],
                                         tokK3, e1u, ALU.mult),
                                     reads=(kte + cte) if u == 0 else [])
                x_cur = 0
                xtok = xve + x0k
                if getattr(self, "_dbg", None) and g == 0 and c == 0:
                    xtok = xtok + self.dma("sp", self._dbg["X0"][:], X[:, 0],
                                           sem="ddbg1", reads=xtok)
                if self.PBLIM <= 5:
                    continue
                pw_bank = 0
                pw_tok = {"1T": tile_toks[0], "1n": nne}
                for pw, srck in [("2", "1"), ("4", "2"), ("8", "4"), ("16", "8"), ("32", "16")]:
                    srcT = NTt if srck == "1" else PW[srck + "T"]
                    srcn = Nn if srck == "1" else PW[srck + "n"]
                    src_tok = pw_tok[srck + "T"] + pw_tok[srck + "n"]
                    for orient in (["T", "n"] if pw != "32" else ["T"]):
                        pbank = pC if pw_bank == 0 else pD
                        bk = "pC" if pw_bank == 0 else "pD"
                        mmk = []
                        for h in range(16):
                            A_ = srcn if orient == "T" else srcT
                            B_ = srcT if orient == "T" else srcn
                            mmk = self.emit("pe", lambda e, h=h, A_=A_, B_=B_, pbank=pbank:
                                            e.matmul(pbank[0:64, h * 64:(h + 1) * 64],
                                                     A_[:, h, :], B_[:, h, :],
                                                     start=True, stop=True),
                                            reads=(src_tok + bank[bk] if h == 0 else []),
                                            incs=[("pe", 1)] if h == 15 else [])
                        dstp = PW[pw + orient]
                        eng = "dve" if pw_bank == 0 else "act"
                        fnc = (lambda e, dstp=dstp, pbank=pbank: e.tensor_copy(
                            dstp[:].rearrange("p h t -> p (h t)"), pbank[0:64, 0:1024])) \
                            if eng == "dve" else \
                            (lambda e, dstp=dstp, pbank=pbank: e.copy(
                                dstp[:].rearrange("p h t -> p (h t)"), pbank[0:64, 0:1024]))
                        ev = self.step(eng, fnc, reads=mmk +
                                       ring_free.get(("PW", pw + orient), []))
                        bank[bk] = ev
                        pw_tok[pw + orient] = ev
                        pw_bank = 1 - pw_bank
                if self.PBLIM <= 6:
                    continue
                fac_order = [("32T", PW["32T"]), ("16T", PW["16T"]), ("8T", PW["8T"]),
                             ("4T", PW["4T"]), ("2T", PW["2T"]), ("NT", NTt)]
                fac_tok = {"NT": tile_toks[0], "32T": pw_tok["32T"],
                           "16T": pw_tok["16T"], "8T": pw_tok["8T"],
                           "4T": pw_tok["4T"], "2T": pw_tok["2T"]}
                apply_mm_last = []
                xcur_tok = xtok
                for fi, (fn, fmat) in enumerate(fac_order):
                    mmk = []
                    for h in range(16):
                        pbank = pC if h < 8 else pD
                        bk = "pC" if h < 8 else "pD"
                        hh = h % 8
                        mmk = self.emit("pe", lambda e, h=h, hh=hh, x_cur=x_cur, pbank=pbank:
                                        e.matmul(pbank[0:64, hh * 128:(hh + 1) * 128],
                                                 self.id128b[0:64, 0:64],
                                                 X[:, x_cur, h, :], start=True, stop=False),
                                        reads=(xcur_tok + bank[bk] if hh == 0 else []), incs=[])
                        mmk = self.emit("pe", lambda e, h=h, hh=hh, fmat=fmat, x_cur=x_cur, pbank=pbank:
                                        e.matmul(pbank[0:64, hh * 128:(hh + 1) * 128],
                                                 fmat[:, h, :],
                                                 X[:, x_cur, h, :], start=False, stop=True),
                                        reads=(fac_tok.get(fn, []) if hh == 0 else []),
                                        incs=[("pe", 1)] if h == 15 or h == 7 else [])
                    ev1 = self.step("dve", lambda e, x_cur=x_cur: e.tensor_copy(
                        X[:, 1 - x_cur, 0:8, :].rearrange("p h w -> p (h w)"),
                        pC[0:64, 0:1024]), reads=mmk)
                    ev2 = self.step("act", lambda e, x_cur=x_cur: e.copy(
                        X[:, 1 - x_cur, 8:16, :].rearrange("p h w -> p (h w)"),
                        pD[0:64, 0:1024]), reads=mmk)
                    bank["pC"] = ev1
                    bank["pD"] = ev2
                    x_cur = 1 - x_cur
                    xcur_tok = ev1 + ev2
                    apply_mm_last = mmk
                X_T = xcur_tok
                ring_free[("PW", "2T")] = apply_mm_last
                if self.PBLIM <= 7:
                    continue
                kwt = []
                for u in range(2):
                    e2u = colT[:, rr, 1, :].rearrange("p (ab u) -> p ab u", u=2)[:, :, u] \
                        .unsqueeze(-1).to_broadcast([64, 8, 64])
                    kwt += self.step("dve", lambda e, u=u, e2u=e2u, tokK3=tokK3:
                                     e.tensor_tensor(
                                         kw[:].rearrange("p (ab u) t -> p ab u t", u=2)[:, :, u, :],
                                         tokK3, e2u, ALU.mult),
                                     reads=(kte + cte + ring_free.get(("kw",), [])) if u == 0 else [])
                ring_free[("kw",)] = []
                wk = []
                for h in range(16):
                    wk = self.emit("pe", lambda e, h=h, x_cur=x_cur: e.matmul(
                        pC[0:64, h * 64:(h + 1) * 64],
                        X[:, x_cur, h, 64:128], kw[:, h, :], start=True, stop=True),
                        reads=(X_T + kwt + bank["pC"] if h == 0 else []),
                        incs=[("pe", 1)] if h == 15 else [])
                nwk = self.step("dve", lambda e: e.tensor_scalar_mul(
                    negWKT[:].rearrange("p h t -> p (h t)"), pC[0:64, :], -1.0), reads=wk)
                bank["pC"] = nwk
                wtt = []
                for h in range(16):
                    wtt = self.emit("pe", lambda e, h=h, x_cur=x_cur: e.transpose(
                        pDb[0:64, h * 64:(h + 1) * 64],
                        X[:, x_cur, h, 64:128], self.id128b[0:64, 0:64]),
                        reads=(X_T + bank["pD"] if h == 0 else []),
                        incs=[("pe", 1)] if h == 15 else [])
                nwt = self.step("act", lambda e: e.activation(
                    negWT[:].rearrange("p h t -> p (h t)"), pDb[0:64, 0:1024],
                    AF.Copy, scale=-1.0), reads=wtt)
                bank["pD"] = nwt
                ch = []
                for h in range(16):
                    pb = pA if h < 8 else pB_
                    bkk = "pA" if h < 8 else "pB"
                    ch += self.emit("pe", lambda e, h=h, pb=pb, x_cur=x_cur: e.matmul(
                        pb[0:64, (h % 8) * 64:(h % 8 + 1) * 64],
                        kw[:, h, :], X[:, x_cur, h, 0:64], start=True, stop=False),
                        reads=(X_T + kwt + bank[bkk] if h % 8 == 0 else []), incs=[])
                    ch += self.emit("pe", lambda e, h=h, pb=pb, sp_old=sp_old: e.matmul(
                        pb[0:64, (h % 8) * 64:(h % 8 + 1) * 64],
                        negWKT[:, h, :], Sbf[0:64, sp_old, h, :], start=False, stop=True),
                        reads=(nwk + schain if h % 8 == 0 else []),
                        incs=[("pe", 1)] if h % 8 == 7 else [])
                e4b = colT[:, rr, 3, :].unsqueeze(-1).to_broadcast([64, 16, 64])
                st1 = self.step("dve", lambda e, sp_old=sp_old, e4b=e4b: e.tensor_tensor(
                    St[:], Ssb[:, sp_old], e4b, ALU.mult), reads=schain + cte)
                sn1 = self.step("dve", lambda e, sp_new=sp_new: e.tensor_tensor(
                    Ssb[:, sp_new, 0:8, :], St[:, 0:8, :],
                    pA[0:64, :].rearrange("p (h t) -> p h t", t=64), ALU.add),
                    reads=ch + st1)
                sn2 = self.step("dve", lambda e, sp_new=sp_new: e.tensor_tensor(
                    Ssb[:, sp_new, 8:16, :], St[:, 8:16, :],
                    pB_[0:64, :].rearrange("p (h t) -> p h t", t=64), ALU.add),
                    reads=ch + st1)
                snb = self.step("dve", lambda e, sp_new=sp_new: e.tensor_copy(
                    Sbf[0:64, sp_new].rearrange("p h t -> p (h t)"),
                    Ssb[:, sp_new].rearrange("p h t -> p (h t)")),
                    reads=sn1 + sn2 + srep[sp_new])
                srep[sp_new] = self.dma("pool", Sbf[64:128, sp_new],
                                        Sbf[0:64, sp_new], sem=f"dsrep{sp_new}",
                                        reads=snb)
                bank["pA"] = sn1
                bank["pB"] = sn2
                if self.PBLIM <= 8:
                    continue
                uc = []
                for h in range(16):
                    uc = self.emit("pe", lambda e, h=h, x_cur=x_cur: e.matmul(
                        pD[0:64, h * 64:(h + 1) * 64],
                        self.id128b[0:64, 0:64], X[:, x_cur, h, 0:64],
                        start=True, stop=False),
                        reads=(X_T + nwt + bank["pD"] if h == 0 else []), incs=[])
                    uc = self.emit("pe", lambda e, h=h, sp_old=sp_old: e.matmul(
                        pD[0:64, h * 64:(h + 1) * 64],
                        negWT[:, h, :], Sbf[0:64, sp_old, h, :], start=False, stop=True),
                        reads=[], incs=[("pe", 1)] if h == 15 else [])
                uce = self.step("act", lambda e: e.copy(
                    Uc[:].rearrange("p h t -> p (h t)"), pD[0:64, 0:1024]), reads=uc)
                bank["pD"] = uce
                if self.PBLIM <= 9:
                    continue
                qs = []
                seenq = {0: False, 1: False}
                for h in range(16):
                    par = (h // 2) % 2
                    fq = h // 4
                    m = h % 2
                    j = h // 4
                    col = (j * 2 + m) * 64
                    bnk = pC if par == 0 else pD
                    firstq = not seenq[par]
                    seenq[par] = True
                    rdq = ((bank["pC"] if par == 0 else bank["pD"]) + lq +
                           srep[sp_old] + uc) if firstq else []
                    qs += self.emit("pe", lambda e, h=h, par=par, fq=fq, r=r, csl=csl, sp_old=sp_old, bnk=bnk, col=col:
                                    e.matmul(bnk[0:64, col:col + 64],
                                             qkvB[par * 64:par * 64 + 64, r, fq, csl],
                                             Sbf[par * 64:par * 64 + 64, sp_old, h, :],
                                             start=True, stop=True),
                                    reads=rdq,
                                    incs=[("pe", 1)])
                o2e = []
                o2e_par = {}
                for par in range(2):
                    bnk = pC if par == 0 else pD
                    e3v = colT[:, rr, 2, :].rearrange("p (j par m) -> p j par m", par=2, m=2)[:, :, par, :] \
                        .unsqueeze(-1).to_broadcast([64, 4, 2, 64])
                    tk = self.step("dve", lambda e, bnk=bnk, par=par, e3v=e3v: e.tensor_tensor(
                        o2sb[:].rearrange("p (j par m) t -> p j par m t", par=2, m=2)[:, :, par, :, :],
                        bnk[0:64, 0:512].rearrange("p (j m t) -> p j m t", m=2, t=64),
                        e3v, ALU.mult), reads=(qs + cte) if par == 0 else [])
                    o2e += tk
                    o2e_par[par] = tk
                bank["pC"] = o2e_par[0]
                bank["pD"] = o2e_par[1]
                ob = (c % 2) * 64
                oo = []
                for h in range(16):
                    oo = self.emit("pe", lambda e, h=h, ob=ob: e.matmul(
                        pE[ob:ob + 64, h * 64:(h + 1) * 64],
                        ATt[:, h, :], Uc[:, h, :], start=True, stop=False,
                        tile_position=(0, ob)),
                        reads=(tile_toks[1] + uce + (bank["pE"] if c % 2 == 0 else []) if h == 0 else []),
                        incs=[])
                    oo = self.emit("pe", lambda e, h=h, ob=ob: e.matmul(
                        pE[ob:ob + 64, h * 64:(h + 1) * 64],
                        self.id128b[0:64, 0:64], o2sb[:, h, :], start=False, stop=True,
                        tile_position=(0, ob)),
                        reads=(o2e if h == 0 else []),
                        incs=[("pe", 1)] if h == 15 else [])
                schain = sn1 + sn2 + snb
                ring_free[("NTt",)] = apply_mm_last
                ring_free[("ATt",)] = oo
                ring_free[("X",)] = uc + wk + wtt + ch
                ring_free[("kw",)] = ch + wk
                ring_free[("colT", rr)] = o2e + st1
                if c % 2 == 1:
                    ring_free[("stg",)] = am[-1:]
                if getattr(self, "_dbg", None) and g == 0 and c == 0:
                    dd = []
                    dd += self.dma("sp", self._dbg["NTt"][:], NTt[:], sem="ddbg2",
                                   reads=tile_toks[0])
                    dd += self.dma("sp", self._dbg["ATt"][:], ATt[:], sem="ddbg3",
                                   reads=tile_toks[1])
                    dd += self.dma("sp", self._dbg["Nn"][:], Nn[:], sem="ddbg4", reads=nne)
                    dd += self.dma("sp", self._dbg["X"][:], X[:, x_cur], sem="ddbg5",
                                   reads=X_T)
                    dd += self.dma("sp", self._dbg["S"][:], Ssb[:, sp_new], sem="ddbg6",
                                   reads=sn1 + sn2)
                    dd += self.dma("sp", self._dbg["colT"][:], colT[:, rr], sem="ddbg7",
                                   reads=cte)
                    dd += self.dma("sp", self._dbg["kw"][:], kw[:], sem="ddbg8", reads=kwt)
                    dd += self.dma("sp", self._dbg["Uc"][:], Uc[:], sem="ddbg9", reads=uce)
                    for key in list(ring_free):
                        ring_free[key] = ring_free[key] + dd
                    schain = schain + dd
                if c % 2 == 0:
                    oo_even = oo
                else:
                    tt2 = c // 2
                    oev = self.step("dve", lambda e: e.tensor_copy(
                        osb[:].rearrange("p h t -> p (h t)"), pE[:, 0:1024]),
                        reads=oo + oo_even)
                    if getattr(self, "_dbg", None) and g == 0 and c == 1:
                        oev = oev + self.dma("sp", self._dbg["osb"][:], osb[:],
                                             sem="ddbg11", reads=oev)
                        self._dump_pair0 = True
                    sqv = self.step("dve", lambda e: e.tensor_tensor(
                        pE[:, 0:1024].rearrange("p (h t) -> p h t", t=64),
                        osb[:], osb[:], ALU.mult), reads=oev)
                    vv = self.step("dve", lambda e: e.tensor_reduce(
                        ovar[:], pE[:, 0:1024].rearrange("p (h t) -> p h t", t=64),
                        AX.X, ALU.add), reads=sqv)
                    bank["pE"] = vv
                    v1 = self.step("act", lambda e: e.activation(
                        ovar[:], ovar[:], AF.Ln, bias=EPS, scale=1.0 / 64.0), reads=vv)
                    v2 = self.step("act", lambda e: e.activation(
                        ovar[:], ovar[:], AF.Exp, scale=-0.5), reads=v1)
                    onn = self.step("dve", lambda e: e.tensor_tensor(
                        osb[:], osb[:], ovar[:].to_broadcast([128, 16, 64]), ALU.mult),
                        reads=v2)
                    on2 = self.step("dve", lambda e: e.tensor_tensor(
                        osb[:], osb[:],
                        self.normw_r[:].unsqueeze(1).to_broadcast([128, 16, 64]),
                        ALU.mult), reads=onn + [self.CONST_T])
                    gt = self.step("dve", lambda e, tt2=tt2: e.tensor_tensor(
                        gated[:, tt2, :].rearrange("p (h t) -> p h t", t=64),
                        osb[:], zB[:, tt2, :].rearrange("p (h t) -> p h t", t=64),
                        ALU.mult), reads=on2 + lz + ring_free.get(("gated",), []))
                    if getattr(self, "_dump_pair0", False):
                        self._dump_pair0 = False
                        gt = gt + self.dma("sp", self._dbg["gated0"][:],
                                           gated[:, 0, :], sem="ddbg12", reads=gt)

                    self._gated_toks = (self._gated_toks + gt) if c > 1 else gt
            if self.PBLIM <= 10:
                continue
            gtt = self._gated_toks
            trd = []
            for i in range(NTT):
                for vcq in range(2):
                    mm = []
                    for k in range(4):
                        vc = vcq * 4 + k
                        pbk, pbkb = (pC, pCb) if vcq == 0 else (pD, pDb)
                        mm = self.emit("pe", lambda e, i=i, vc=vc, k=k, pbkb=pbkb:
                                       e.transpose(pbkb[:, k * 128:(k + 1) * 128],
                                                   gated[:, i, vc * 128:(vc + 1) * 128],
                                                   self.id128b[:]),
                                       reads=(gtt + bank["pC" if vcq == 0 else "pD"]
                                              if k == 0 else []),
                                       incs=[("pe", 1)] if k == 3 else [])
                    ev = self.step("act", lambda e, i=i, vcq=vcq,
                                   pbkb=(pCb if vcq == 0 else pDb): e.copy(
                        gatedT[:, vcq * 4:vcq * 4 + 4, i * 128:(i + 1) * 128],
                        pbkb[:, 0:512].rearrange("p (a b) -> p a b", b=128)),
                        reads=mm + (ring_free.get(("gatedT",), []) if i == 0 and vcq == 0 else []))
                    bank["pC" if vcq == 0 else "pD"] = ev
                    trd += ev
            gated_tr_free = trd[-1:]
            res_toks = []
            for i in range(NTT):
                for nh in range(2):
                    pbk = pC if nh == 0 else pD
                    bkk = "pC" if nh == 0 else "pD"
                    mm = []
                    for vc in range(8):
                        mm = self.emit("pe", lambda e, i=i, nh=nh, vc=vc, pbk=pbk:
                                       e.matmul(pbk[:, 0:512],
                                                gatedT[:, vc, i * 128:(i + 1) * 128],
                                                wout[:, vc, nh * 512:(nh + 1) * 512],
                                                start=(vc == 0), stop=(vc == 7)),
                                       reads=(trd + bank[bkk] if vc == 0 else []) +
                                             ([W_T] if g == 0 and i == 0 and vc == 0 else []),
                                       incs=[("pe", 1)] if vc == 7 else [])
                    ev = self.step("dve", lambda e, i=i, nh=nh, pbk=pbk:
                                   e.scalar_tensor_tensor(
                                       xg2[:, i, nh * 512:(nh + 1) * 512],
                                       pbk[:, 0:512], 1.0,
                                       xg2[:, i, nh * 512:(nh + 1) * 512],
                                       ALU.mult, ALU.add),
                                   reads=mm + lx)
                    bank[bkk] = ev
                    res_toks += ev
            sto = self.dma("sp", self.out_d[gsl, :].rearrange("(i p) d -> p i d", p=128),
                           xg2[:], sem="dout", reads=res_toks)
            self.res_done.append(sto)
            ln_toks = []
            for i in range(NTT):
                ln_toks += self.step("dve", lambda e, i=i: e.tensor_reduce(
                    lnstat[:, 0, i:i + 1], xg2[:, i, :], AX.X, ALU.add),
                    reads=res_toks + [self.tok1("dve")])
                ln_toks += self.step("act", lambda e, i=i: e.activation(
                    scrap[:, 0:D], xg2[:, i, :], AF.Square,
                    accum_out=lnstat[:, 1, i:i + 1]),
                    reads=res_toks + [self.tok1("act")] + [self.tok1("dve")])
            t1 = self.step("dve", lambda e: e.tensor_scalar_mul(
                mu[:], lnstat[:, 0, :], 1.0 / D), reads=ln_toks)
            t2 = self.step("dve", lambda e: e.scalar_tensor_tensor(
                rsq[:], mu[:], -1.0, mu[:], ALU.mult, ALU.mult), reads=t1)
            t3 = self.step("dve", lambda e: e.scalar_tensor_tensor(
                rsq[:], lnstat[:, 1, :], 1.0 / D, rsq[:], ALU.mult, ALU.add),
                reads=t2 + ln_toks)
            t4 = self.step("act", lambda e: e.activation(
                rsq[:], rsq[:], AF.Ln, bias=LNEPS), reads=t3)
            t5 = self.step("act", lambda e: e.activation(
                rsq[:], rsq[:], AF.Exp, scale=-0.5), reads=t4)
            x2_toks = []
            for i in range(NTT):
                x2_toks += self.step("dve", lambda e, i=i: e.tensor_scalar(
                    gated[:, i, :].bitcast(BF16)[:, 0:D] if False else gated[:, i, :],
                    xg2[:, i, :], mu[:, i:i + 1], rsq[:, i:i + 1],
                    ALU.subtract, ALU.mult),
                    reads=t5 + (ring_free.get(("gated",), []) if i == 0 else []))
            ring_free[("xg2",)] = x2_toks[-1:] + ln_toks[-1:] + sto
            stx = self.dma("pool", self.xln2_h[gsl, :].rearrange("(i p) d -> p i d", p=128),
                           gated[:], sem="dxl2", reads=x2_toks)
            self.xln2_done = stx
            trd2 = []
            for i in range(NTT):
                for dcq in range(2):
                    pbk, pbkb = (pC, pCb) if dcq == 0 else (pD, pDb)
                    bkk = "pC" if dcq == 0 else "pD"
                    mm = []
                    for k in range(4):
                        dc = dcq * 4 + k
                        mm = self.emit("pe", lambda e, i=i, dc=dc, k=k, pbkb=pbkb:
                                       e.transpose(pbkb[:, k * 128:(k + 1) * 128],
                                                   gated[:, i, dc * 128:(dc + 1) * 128],
                                                   self.id128b[:]),
                                       reads=(x2_toks + bank[bkk] if k == 0 else []),
                                       incs=[("pe", 1)] if k == 3 else [])
                    ev = self.step("act", lambda e, i=i, dcq=dcq,
                                   pbkb=(pCb if dcq == 0 else pDb): e.copy(
                        gatedT[:, dcq * 4:dcq * 4 + 4, i * 128:(i + 1) * 128],
                        pbkb[:, 0:512].rearrange("p (a b) -> p a b", b=128)),
                        reads=mm + (ring_free.get(("gatedT",), []) if i == 0 and dcq == 0 else []))
                    bank[bkk] = ev
                    trd2 += ev
            ring_free[("gated",)] = trd2[-1:] + [stx[-1]]
            mm = []
            for dc in range(8):
                mm = self.emit("pe", lambda e, dc=dc: e.matmul(
                    pA[0:16, :], gqt[:, dc, :], gatedT[:, dc, :],
                    start=(dc == 0), stop=(dc == 7)),
                    reads=(trd2 + bank["pA"] if dc == 0 else []),
                    incs=[("pe", 1)] if dc == 7 else [])
            ring_free[("gatedT",)] = mm[-1:]
            lge = self.step("dve", lambda e: e.tensor_scalar_mul(
                lgs[:], pA[0:16, :], eitemp[0:16, :]),
                reads=mm + eit + (ring_free.get(("lgs",), [])))
            bank["pA"] = lge
            stl = self.dma("pool", self.logit_h[:, gsl], lgs[:], sem="dlg", reads=lge)
            self.logit_done = stl
            ring_free[("lgs",)] = stl
            ring_free[("qkvB", r)] = [self.tok1("pe")]
            ring_free[("zB",)] = self._gated_toks[-1:]
            ring_free[("bgB",)] = csx[-1:] + spl_done[-1:]
            ring_free[("lnkq",)] = spl_done[-1:]
        self.barrier()
        self.scope.close()
        self.scope = None

    def pass_c(self):
        T, NG = self.T, self.NG
        NT8 = T // 8
        TILE = min(128, NT8)
        NT = T // TILE
        TPB = NT8 // TILE
        NW = min((512 + TILE - 1) // TILE, NT)
        PC = (16 * NT + 127) // 128
        FINW = 384
        DUMP = 16 * FINW
        FINTOT = ((16 * FINW + TILE + 127) // 128) * 128
        HUGE = 65536.0
        CSTEPS = []
        _s = 1
        while _s < NT8:
            CSTEPS.append(_s)
            _s *= 2
        self.scope = contextlib.ExitStack()
        sb = self.sbuf
        d = self.dram

        self.din("zarr", [128, NT8], I32)
        self.din("iota32p", [TILE, 32])
        self.din("iotawp", [TILE, TILE])
        self.din("nhl", [TILE, 2], BF16)
        self.din("mstrict", [128, 128])
        self.din("tc32", [32, NT])
        self.din("tcw", [TILE, NW])
        self.din("econ2", [16, 1])
        self.dint("fin_h", [FINTOT])
        self.dint("b2_h", [16, NT])
        self.dint("aff_h", [16, T])
        aff_flat = d["aff_h"].rearrange("e t -> (e t)").unsqueeze(-1)
        fin_flat = d["fin_h"].unsqueeze(-1)

        aff = sb("aff", [128, NT8])
        affw = sb("affw", [128, NT8])
        selb = sb("selb", [128, NT8])
        seli = sb("seli", [128, NT8], I32)
        keyp = sb("keyp", [128, NT8], I32)
        red = sb("red", [128, 2])
        lse8 = sb("lse8", [8, NT8])
        lseE = sb("lseE", [128, 1])
        lse16 = sb("lse16", [16, 1])
        onesr = sb("onesr", [1, 128])
        halves = sb("halves", [16, 4])
        flagg = sb("flagg", [16, 1])
        cnti = sb("cnti", [128, 1], I32)
        cntf = sb("cntf", [128, 1])
        trialf = sb("trialf", [128, 2])
        scrb = sb("scrb", [128, NT8])
        cnt16 = sb("cnt16", [16, 1])
        base = sb("base", [128, 1])
        barr = sb("barr", [128, TPB])
        zar = sb("zar", [128, NT8], I32)
        iota32p = sb("iota32p", [TILE, 32])
        iotawp = sb("iotawp", [TILE, TILE])
        nhl = sb("nhl", [TILE, 2], BF16)
        mstrict = sb("mstrict", [128, 128])
        tc32 = sb("tc32", [32, NT])
        tcw = sb("tcw", [TILE, NW])
        econ2 = sb("econ2", [16, 1])
        rank_all = sb("rank_all", [TILE, 16, NT])
        rtokc = sb("rtokc", [TILE, NT])
        btok = sb("btok", [TILE, NT])
        clb = sb("clb", [TILE, NT])
        dbuf = sb("dbuf", [TILE, max(NT * 32, NW * TILE)])
        oh32 = sb("oh32", [TILE, NT * 32], BF16)
        ohw_all = sb("ohw", [TILE, 16, NW * TILE], BF16)
        cand32 = sb("cand32", [32, 16 * NT * 2])
        candw = sb("candw", [TILE, 16 * NW * 2])
        idx32sb = sb("idx32sb", [32, 16 * NT])
        idxwsb = sb("idxwsb", [TILE, 16 * NW])
        pay32 = sb("pay32", [128, PC, 32])
        payw = sb("payw", [16 * NW, TILE])
        b2sb = sb("b2sb", [16, NT])
        destT_f = sb("destT_f", [16, NT])
        destT = sb("destT", [16, NT], I32)
        destW_f = sb("destW_f", [16, NW])
        destW = sb("destW", [16, NW], I32)
        zsbuf = sb("zsbuf", [128, FINTOT // 128])
        idxf = sb("idxf", [128, 2, 2])
        idxi = sb("idxi", [128, 2, 2], I32)
        idxg = sb("idxg", [128, 2, 2], I32)
        gts = sb("gts", [128, 2, 2])
        xg32 = sb("xg32", [128, 2, 2, D])
        xbf = sb("xbf", [128, 2, D], BF16)
        w1t = sb("w1t", [128, 4, 8, 128], BF16)
        w2t = sb("w2t", [128, 8, D], BF16)
        xe = sb("xe", [128, 2, 8, CAP], BF16)
        hT = sb("hT", [128, 2, 32, CAP], BF16)
        b1s = sb("b1s", [128, 2, 32])
        b2r = sb("b2r", [128, 2, D])
        osb2 = sb("osb2", [128, 2, 2, D])

        pA = self.psum("cA", [128, 512])
        pB_ = self.psum("cB", [128, 512])
        pH = self.psum("cH", [128, 512])
        pX = self.psum("cX", [128, 512])
        pO1 = self.psum("cO1", [128, 1024])
        pO2 = self.psum("cO2", [128, 1024])
        pHb = pH[:].bitcast(BF16)

        bank = {}
        cl_toks = []
        for t_, nm in [(zar, "zarr"), (iota32p, "iota32p"), (iotawp, "iotawp"),
                       (nhl, "nhl"), (mstrict, "mstrict"), (tc32, "tc32"),
                       (tcw, "tcw"), (econ2, "econ2")]:
            cl_toks += self.dma("sp", t_[:], d[nm][:], sem="cst")
        one_t = self.step("dve", lambda e: e.memset(onesr[:], 1.0))
        hz = self.step("dve", lambda e: e.memset(halves[:], 0.0))
        zz = self.step("dve", lambda e: e.memset(zsbuf[:], 0.0))
        zfin = self.dma("pool", d["fin_h"].rearrange("(p k) -> p k", p=128),
                        zsbuf[:], sem="czf", reads=zz)

        lsrc = d["logit_h"][:].rearrange("e (tb f) -> tb e f", tb=8)
        ll = self.dma("sp", aff[:], lsrc, sem="cl0", reads=[self.tok1("dlg")])
        m1 = self.step("dve", lambda e: e.tensor_reduce(
            red[:, 0:1], aff[:], AX.X, ALU.max), reads=ll)
        m2a = self.emit("pe", lambda e: e.transpose(
            pA[0:1, 0:128], red[:, 0:1], self.id128f[:]),
            reads=m1 + [self.CONST_T], incs=[("pe", 1)])
        m2b = self.step("dve", lambda e: e.tensor_reduce(
            red[0:1, 1:2], pA[0:1, 0:128], AX.X, ALU.max), reads=m2a)
        m2c = self.emit("pe", lambda e: e.matmul(
            pB_[:, 0:1], onesr[:], red[0:1, 1:2], start=True, stop=True),
            reads=m2b + one_t, incs=[("pe", 1)])
        m2d = self.step("dve", lambda e: e.tensor_copy(red[:, 1:2], pB_[:, 0:1]),
                        reads=m2c)
        m3 = self.step("dve", lambda e: e.tensor_scalar_sub(
            aff[:], aff[:], red[:, 1:2]), reads=m2d)
        bank["pA"] = m2b
        bank["pB"] = m2d
        prev = m3
        segs = [(o, min(512, NT8 - o)) for o in range(0, NT8, 512)]
        for it in range(6):
            ex1 = self.step("act", lambda e: e.activation(affw[:], aff[:], AF.Exp),
                            reads=prev + bank.get("affw", []))
            mm1 = []
            for nh, (of, wd) in enumerate(segs):
                pb = pA if nh == 0 else pB_
                mm1 += self.step("pe", lambda e, of=of, wd=wd, pb=pb: e.matmul(
                    pb[0:8, 0:wd], self.blk16[:],
                    affw[:, of:of + wd], start=True, stop=True),
                    reads=ex1 + bank.get("pA", []) + bank.get("pB", []))
            ev1 = []
            for nh, (of, wd) in enumerate(segs):
                pb = pA if nh == 0 else pB_
                ev1 += self.step("dve", lambda e, of=of, wd=wd, pb=pb: e.tensor_copy(
                    lse8[:, of:of + wd], pb[0:8, 0:wd]), reads=mm1)
            ln1 = self.step("act", lambda e: e.activation(
                lse8[:], lse8[:], AF.Ln), reads=ev1)
            mm2 = []
            for nh, (of, wd) in enumerate(segs):
                pb = pA if nh == 0 else pB_
                mm2 += self.step("pe", lambda e, of=of, wd=wd, pb=pb: e.matmul(
                    pb[:, 0:wd], self.exp16[:], lse8[:, of:of + wd],
                    start=True, stop=True), reads=ln1 + ev1)
            s1 = []
            for nh, (of, wd) in enumerate(segs):
                pb = pA if nh == 0 else pB_
                s1 += self.step("dve", lambda e, of=of, wd=wd, pb=pb: e.tensor_sub(
                    aff[:, of:of + wd], aff[:, of:of + wd],
                    pb[:, 0:wd]), reads=mm2)
            bank["pA"] = s1
            bank["pB"] = s1
            ex2 = self.step("act", lambda e: e.activation(affw[:], aff[:], AF.Exp),
                            reads=s1)
            bank["affw"] = ex2
            r1 = self.step("dve", lambda e: e.tensor_reduce(
                red[:, 0:1], affw[:], AX.X, ALU.add), reads=ex2 + m2d)
            mm3 = self.step("pe", lambda e: e.matmul(
                pA[0:16, 0:1], self.comb16[:], red[:, 0:1], start=True, stop=True),
                reads=r1 + bank.get("pA", []))
            ln2 = self.step("act", lambda e: e.activation(
                lse16[:], pA[0:16, 0:1], AF.Ln),
                reads=mm3 + bank.get("lse16", []))
            bank["lse16"] = ln2
            mm4 = self.step("pe", lambda e: e.matmul(
                pB_[:, 0:1], self.expc16[:], lse16[:], start=True, stop=True),
                reads=ln2 + bank.get("pB", []))
            cp1 = self.step("dve", lambda e: e.tensor_copy(lseE[:], pB_[:, 0:1]),
                            reads=mm4)
            s2 = self.step("dve", lambda e: e.tensor_scalar_sub(
                aff[:], aff[:], lseE[:]), reads=cp1 + s1)
            bank["pA"] = ln2
            bank["pB"] = cp1
            prev = s2
        exf = self.step("act", lambda e: e.activation(affw[:], aff[:], AF.Exp),
                        reads=prev + bank.get("affw", []))
        sta = self.dma("pool", d["aff_h"][:].rearrange("e (tb f) -> tb e f", tb=8),
                       affw[:], sem="ca1", reads=exf + zfin)

        if self.PCLIM <= 1:
            self.barrier()
            self.scope.close()
            self.scope = None
            return
        q1 = self.step("dve", lambda e: e.tensor_single_scalar(
            seli[:], affw[:].bitcast(I32), 1, ALU.is_ge),
            reads=exf + cl_toks + sta + [self.tok1("dve")])
        q2 = self.step("dve", lambda e: e.tensor_single_scalar(
            seli[:], seli[:], -1, ALU.mult), reads=q1)
        q3 = self.step("dve", lambda e: e.tensor_tensor(
            keyp[:], affw[:].bitcast(I32), seli[:], ALU.bitwise_and), reads=q2)
        q4 = self.step("dve", lambda e: e.tensor_single_scalar(
            seli[:], seli[:], -1, ALU.bitwise_xor), reads=q3)
        q5 = self.step("dve", lambda e: e.tensor_tensor(
            seli[:], seli[:], zar[:], ALU.bitwise_and), reads=q4)
        k5 = self.step("dve", lambda e: e.tensor_tensor(
            keyp[:], keyp[:], seli[:], ALU.bitwise_or), reads=q5)

        GTb = aff
        bitf = affw
        Mb = selb
        minit = self.step("dve", lambda e: e.memset(Mb[:], 1.0), reads=k5 + sta)
        ginit = self.step("dve", lambda e: e.memset(GTb[:], 0.0), reads=sta)
        chain = minit + ginit + hz
        for kk in range(30, -1, -1):
            s1 = self.step("dve", lambda e, kk=kk: e.tensor_single_scalar(
                seli[:], keyp[:], kk, ALU.logical_shift_right), reads=chain + k5)
            s2 = self.step("dve", lambda e: e.tensor_single_scalar(
                seli[:], seli[:], 1, ALU.bitwise_and), reads=s1)
            s3 = self.step("dve", lambda e: e.tensor_copy(bitf[:], seli[:]),
                           reads=s2)
            s4 = self.step("dve", lambda e: e.tensor_mul(
                scrb[:], Mb[:], bitf[:]), reads=s3)
            s5 = self.step("dve", lambda e: e.tensor_reduce(
                cntf[:], scrb[:], AX.X, ALU.add), reads=s4)
            mmc = self.step("pe", lambda e: e.matmul(
                pA[0:16, 0:1], self.comb16[:], cntf[:], start=True, stop=True),
                reads=s5 + bank.get("pA", []))
            c7 = self.step("dve", lambda e: e.tensor_copy(cnt16[:], pA[0:16, 0:1]),
                           reads=mmc)
            bank["pA"] = c7
            f1 = self.step("dve", lambda e: e.tensor_add(
                flagg[:], cnt16[:], halves[:, 0:1]), reads=c7)
            f2 = self.step("dve", lambda e: e.tensor_single_scalar(
                flagg[:], flagg[:], 255.5, ALU.is_ge), reads=f1)
            g1 = self.step("dve", lambda e: e.tensor_mul(
                halves[:, 1:2], cnt16[:], flagg[:]), reads=f2)
            g2 = self.step("dve", lambda e: e.tensor_add(
                halves[:, 0:1], halves[:, 0:1], cnt16[:]), reads=g1)
            g3 = self.step("dve", lambda e: e.tensor_sub(
                halves[:, 0:1], halves[:, 0:1], halves[:, 1:2]), reads=g2)
            mmb = self.step("pe", lambda e: e.matmul(
                pB_[:, 0:1], self.expc16[:], flagg[:], start=True, stop=True),
                reads=f2 + bank.get("pB", []))
            fb = self.step("dve", lambda e: e.tensor_copy(
                trialf[:, 0:1], pB_[:, 0:1]), reads=mmb)
            bank["pB"] = fb
            fbb = trialf[:, 0:1].to_broadcast([128, NT8])
            u1 = self.step("dve", lambda e, fbb=fbb: e.tensor_tensor(
                bitf[:], scrb[:], fbb, ALU.mult), reads=fb + s4)
            u2 = self.step("dve", lambda e: e.tensor_add(
                GTb[:], GTb[:], scrb[:]), reads=u1)
            u3 = self.step("dve", lambda e: e.tensor_sub(
                GTb[:], GTb[:], bitf[:]), reads=u2)
            m1 = self.step("dve", lambda e: e.scalar_tensor_tensor(
                bitf[:], scrb[:], 2.0, Mb[:], ALU.mult, ALU.subtract),
                reads=u3)
            m2 = self.step("dve", lambda e, fbb=fbb: e.tensor_tensor(
                bitf[:], bitf[:], fbb, ALU.mult), reads=m1)
            m3 = self.step("dve", lambda e: e.tensor_sub(
                Mb[:], Mb[:], scrb[:]), reads=m2)
            m4 = self.step("dve", lambda e: e.tensor_add(
                Mb[:], Mb[:], bitf[:]), reads=m3)
            chain = m4 + g3
        sbf = self.step("dve", lambda e: e.tensor_add(
            selb[:], selb[:], GTb[:]), reads=chain)

        if self.PCLIM <= 2:
            self.barrier()
            self.scope.close()
            self.scope = None
            return
        bufs = [selb, aff, affw]
        cur = 0
        chain2 = sbf + sta + exf
        for si, ss in enumerate(CSTEPS):
            sv = bufs[cur]
            nxt = 1 + (si % 2)
            dv = bufs[nxt]
            a1 = self.step("dve", lambda e, sv=sv, dv=dv, ss=ss: e.tensor_tensor(
                dv[:, ss:], sv[:, ss:], sv[:, 0:NT8 - ss], ALU.add),
                reads=chain2)
            a2 = self.step("dve", lambda e, sv=sv, dv=dv, ss=ss: e.tensor_copy(
                dv[:, 0:ss], sv[:, 0:ss]))
            chain2 = a1 + a2
            cur = nxt
        c1buf = bufs[cur]
        slotbuf = bufs[2 if cur == 1 else 1]
        mmd = self.step("pe", lambda e: e.matmul(
            pA[:, 0:1], mstrict[:], c1buf[:, NT8 - 1:NT8], start=True, stop=True),
            reads=chain2 + cl_toks + bank.get("pA", []))
        be = self.step("dve", lambda e: e.tensor_copy(base[:], pA[:, 0:1]),
                       reads=mmd)
        bank["pA"] = be
        r1 = self.step("dve", lambda e: e.tensor_scalar_add(
            slotbuf[:], c1buf[:], base[:, 0:1]), reads=be + chain2)
        r2 = self.step("dve", lambda e: e.tensor_scalar_add(
            slotbuf[:], slotbuf[:], -HUGE), reads=r1)
        r3 = self.step("dve", lambda e: e.tensor_mul(
            slotbuf[:], slotbuf[:], selb[:]), reads=r2)
        r4 = self.step("dve", lambda e: e.tensor_scalar_add(
            slotbuf[:], slotbuf[:], HUGE), reads=r3)
        bc0 = self.step("dve", lambda e: e.tensor_copy(barr[:, 0:1], base[:]),
                        reads=be)
        ball = bc0
        if TPB > 1:
            cv3 = c1buf[:].rearrange("p (i t) -> p i t", t=TILE)
            bc1 = self.step("dve", lambda e, cv3=cv3: e.tensor_scalar_add(
                barr[:, 1:TPB], cv3[:, 0:TPB - 1, TILE - 1], base[:, 0:1]),
                reads=be + chain2 + bc0)
            ball = bc0 + bc1
        st_b2 = []
        for tb in range(8):
            st_b2 += self.dma("pool", d["b2_h"][:, tb * TPB:(tb + 1) * TPB],
                              barr[tb * 16:(tb + 1) * 16, :], sem="cb2",
                              reads=(ball + sta) if tb == 0 else [],
                              allow_slow_non_contiguous=True)

        if self.PCLIM <= 3:
            self.barrier()
            self.scope.close()
            self.scope = None
            return
        ra_toks = []
        rank4 = rank_all[:].rearrange("p e (tb i) -> p e tb i", i=TPB)
        HB = min(4, TPB)
        for g in range((TPB + HB - 1) // HB):
            trk = []
            for j in range(HB):
                i = g * HB + j
                if i >= TPB:
                    break
                trk += self.emit("pe", lambda e, i=i, j=j, g=g: e.transpose(
                    pO2[0:TILE, (g % 2) * 512 + j * 128:(g % 2) * 512 + (j + 1) * 128],
                    slotbuf[:, i * TILE:(i + 1) * TILE], self.id128f[:]),
                    reads=(r4 + bank.get("pO2", []) + ra_toks if j == 0 else []),
                    incs=[("pe", 1)])
            nb = min(HB, TPB - g * HB)
            ev = self.step("dve", lambda e, g=g, nb=nb: e.tensor_copy(
                rank4[:, :, :, g * HB:g * HB + nb].rearrange("p e tb i -> p i tb e"),
                pO2[0:TILE, (g % 2) * 512:(g % 2) * 512 + nb * 128]
                .rearrange("p (i tb ee) -> p i tb ee", ee=16, tb=8)),
                reads=trk)
            ra_toks += ev
        bank["pO2"] = ra_toks

        if self.PCLIM <= 35:
            self.barrier()
            self.scope.close()
            self.scope = None
            return
        dl1 = self.dma("sp", b2sb[:], d["b2_h"][:], sem="cd1", reads=st_b2)
        d1 = self.step("dve", lambda e: e.tensor_scalar_min(
            destT_f[:], b2sb[:], float(FINW - 32)), reads=dl1)
        d2 = self.step("dve", lambda e: e.tensor_scalar_add(
            destT_f[:], destT_f[:], econ2[:, 0:1]), reads=d1 + cl_toks)
        d3 = self.step("dve", lambda e: e.tensor_copy(destT[:], destT_f[:]),
                       reads=d2)
        d4 = self.step("dve", lambda e: e.tensor_scalar_min(
            destW_f[:], b2sb[:, 0:NW], float(FINW - TILE)), reads=dl1)
        d5 = self.step("dve", lambda e: e.tensor_scalar_add(
            destW_f[:], destW_f[:], econ2[:, 0:1]), reads=d4 + cl_toks)
        d6 = self.step("dve", lambda e: e.tensor_copy(destW[:], destW_f[:]),
                       reads=d5)

        if self.PCLIM <= 4:
            self.barrier()
            self.scope.close()
            self.scope = None
            return
        PBK = [pA, pB_, pH, pX]
        prev_oh = []
        prev_mm32 = {0: [], 1: [], 2: [], 3: []}
        prev_mmw = []
        bload_prev = []
        cev = {}
        for e_ in range(16):
            gslot = e_ // 4
            pbank = PBK[gslot]
            rtok = rank_all[:, e_:e_ + 1, :].rearrange("p a t -> p (a t)")
            lb = self.dma("act", btok[:], d["b2_h"][e_:e_ + 1, :]
                          .partition_broadcast(TILE), sem="crb",
                          reads=st_b2 + bload_prev)
            c0o = self.step("dve", lambda e, rtok=rtok: e.tensor_copy(
                rtokc[:], rtok),
                reads=ra_toks + prev_oh + [self.tok1("dve")])
            c1o = self.step("dve", lambda e: e.scalar_tensor_tensor(
                clb[:], btok[:], -1.0, rtokc[:], ALU.mult, ALU.add),
                reads=lb + c0o)
            bload_prev = c1o
            dv3 = dbuf[:, 0:NT * 32].rearrange("p (t j) -> p t j", j=32)
            n1 = self.step("dve", lambda e, dv3=dv3: e.scalar_tensor_tensor(
                dv3, clb[:].unsqueeze(-1).to_broadcast([TILE, NT, 32]), -1.0,
                iota32p[:].unsqueeze(1).to_broadcast([TILE, NT, 32]),
                ALU.mult, ALU.add), reads=c1o + cl_toks + prev_oh)
            n2 = self.step("act", lambda e: e.activation(
                dbuf[:, 0:NT * 32], dbuf[:, 0:NT * 32], AF.Square), reads=n1)
            n3 = self.step("act", lambda e: e.activation(
                oh32[:], dbuf[:, 0:NT * 32], AF.Relu, bias=1.0, scale=-1.0),
                reads=n2 + prev_mm32[gslot] + (prev_mm32[(gslot + 3) % 4] if e_ > 0 else []))
            wv3 = dbuf[:, 0:NW * TILE].rearrange("p (t j) -> p t j", j=TILE)
            w1o = self.step("dve", lambda e, wv3=wv3: e.scalar_tensor_tensor(
                wv3, clb[:, 0:NW].unsqueeze(-1).to_broadcast([TILE, NW, TILE]),
                -1.0, iotawp[:].unsqueeze(1).to_broadcast([TILE, NW, TILE]),
                ALU.mult, ALU.add), reads=n3 + c1o + cl_toks)
            w2o = self.step("act", lambda e: e.activation(
                dbuf[:, 0:NW * TILE], dbuf[:, 0:NW * TILE], AF.Square),
                reads=w1o)
            ohw = ohw_all[:, e_, :]
            w3o = self.step("act", lambda e, ohw=ohw: e.activation(
                ohw, dbuf[:, 0:NW * TILE], AF.Relu, bias=1.0, scale=-1.0),
                reads=w2o + prev_mmw)
            prev_oh = w3o
            mm = []
            for t in range(NT):
                mm = self.emit("pe", lambda e, t=t, e_=e_, pbank=pbank: e.matmul(
                    pbank[0:32, (e_ % 4) * NT * 2 + t * 2:(e_ % 4) * NT * 2 + t * 2 + 2],
                    oh32[:, t * 32:(t + 1) * 32], nhl[:], start=True, stop=True),
                    reads=(n3 + cl_toks + (cev.get(gslot, []) if e_ % 4 == 0 else []))
                    if t == 0 else [],
                    incs=[("pe", 1)])
            prev_mm32[gslot] = mm
            mw = []
            for t in range(NW):
                mw = self.emit("pe", lambda e, t=t, e_=e_, ohw=ohw: e.matmul(
                    pO1[0:TILE, (t * 16 + e_) * 2:(t * 16 + e_) * 2 + 2],
                    ohw[:, t * TILE:(t + 1) * TILE], nhl[:], start=True, stop=True),
                    reads=(w3o + (bank.get("pO1", []) if e_ == 0 else [])) if t == 0 else [],
                    incs=[("pe", 1)])
            prev_mmw = mw
            if e_ % 4 == 3:
                c32v4 = cand32[:].rearrange("s (t ee two) -> s t ee two", ee=16, two=2)
                ev = self.step("dve", lambda e, gslot=gslot, pbank=pbank, c32v4=c32v4:
                               e.tensor_copy(
                    c32v4[:, :, gslot * 4:(gslot + 1) * 4, :],
                    pbank[0:32, 0:8 * NT].rearrange("s (ee t two) -> s t ee two",
                                                    ee=4, two=2)), reads=mm)
                cev[gslot] = ev
        evw = self.step("dve", lambda e: e.tensor_copy(
            candw[:], pO1[0:TILE, 0:16 * NW * 2]), reads=prev_mmw)
        bank["pO1"] = evw

        if self.PCLIM <= 5:
            self.barrier()
            self.scope.close()
            self.scope = None
            return
        c32v = cand32[:].rearrange("s (q two) -> s q two", two=2)
        rc1 = self.step("dve", lambda e, c32v=c32v: e.scalar_tensor_tensor(
            idx32sb[:], c32v[:, :, 0], 64.0, c32v[:, :, 1], ALU.mult, ALU.add),
            reads=[t for v in cev.values() for t in v])
        i32v = idx32sb[:].rearrange("s (t e) -> s t e", e=16)
        rc2 = self.step("dve", lambda e, i32v=i32v: e.tensor_tensor(
            i32v, i32v, tc32[:].unsqueeze(-1).to_broadcast([32, NT, 16]), ALU.add),
            reads=rc1 + cl_toks)
        rc3 = self.step("dve", lambda e, c32v=c32v: e.tensor_scalar_min(
            c32v[:, :, 0], c32v[:, :, 1], 1.0), reads=rc1)
        rc4 = self.step("dve", lambda e, c32v=c32v: e.tensor_mul(
            idx32sb[:], idx32sb[:], c32v[:, :, 0]), reads=rc2 + rc3)
        cwv = candw[:].rearrange("s (q two) -> s q two", two=2)
        rw1 = self.step("dve", lambda e, cwv=cwv: e.scalar_tensor_tensor(
            idxwsb[:], cwv[:, :, 0], 64.0, cwv[:, :, 1], ALU.mult, ALU.add),
            reads=evw)
        iwv = idxwsb[:].rearrange("s (t e) -> s t e", e=16)
        rw2 = self.step("dve", lambda e, iwv=iwv: e.tensor_tensor(
            iwv, iwv, tcw[:].unsqueeze(-1).to_broadcast([TILE, NW, 16]), ALU.add),
            reads=rw1 + cl_toks)
        rw3 = self.step("dve", lambda e, cwv=cwv: e.tensor_scalar_min(
            cwv[:, :, 0], cwv[:, :, 1], 1.0), reads=rw1)
        rw4 = self.step("dve", lambda e, cwv=cwv: e.tensor_mul(
            idxwsb[:], idxwsb[:], cwv[:, :, 0]), reads=rw2 + rw3)

        trp = []
        for c in range(PC):
            trp = self.emit("pe", lambda e, c=c: e.transpose(
                pO2[0:128, c * 32:(c + 1) * 32],
                idx32sb[:, c * 128:(c + 1) * 128], self.id128f[0:32, 0:32]),
                reads=(rc4 + bank.get("pO2", []) if c == 0 else []),
                incs=[("pe", 1)])
        tv = self.step("act", lambda e: e.copy(
            pay32[:].rearrange("p c j -> p (c j)"), pO2[0:128, 0:PC * 32]),
            reads=trp)
        bank["pO2"] = tv
        trw = self.emit("pe", lambda e: e.transpose(
            pO2[0:16 * NW, 512:512 + TILE], idxwsb[:],
            self.id128f[0:TILE, 0:TILE]),
            reads=rw4 + tv, incs=[("pe", 1)])
        twv = self.step("act", lambda e: e.copy(
            payw[:], pO2[0:16 * NW, 512:512 + TILE]), reads=trw)
        bank["pO2"] = bank["pO2"] + twv

        if self.PCLIM <= 6:
            self.barrier()
            self.scope.close()
            self.scope = None
            return
        sc_chain = []
        for t in range(NW, NT):
            c, pr = t // 8, t % 8
            sc_chain = self.emit("pool", lambda e, t=t, c=c, pr=pr: e.indirect_dma_start(
                fin_flat, IOA(ap=destT[:, t:t + 1], axis=0),
                pay32[pr * 16:(pr + 1) * 16, c, :], None, compute_op=ALU.add),
                reads=tv + d3 + zfin + sc_chain, incs=[("cfs", 16)])
        for t in range(NW):
            sc_chain = self.emit("pool", lambda e, t=t: e.indirect_dma_start(
                fin_flat, IOA(ap=destW[:, t:t + 1], axis=0),
                payw[t * 16:(t + 1) * 16, :], None, compute_op=ALU.add),
                reads=twv + d6 + zfin + sc_chain, incs=[("cfs", 16)])
        comp_done = sc_chain

        if self.PCLIM <= 7:
            self.barrier()
            self.scope.close()
            self.scope = None
            return
        sc_prev = []
        gx_prev = []
        gg_prev = []
        xe_free = [[], []]
        hT_free = [[], []]
        os_free = [[], []]
        b_free = [[], []]
        idx_free = [[], []]
        xg_free = [[], []]
        w1slot_free = {s: [] for s in range(4)}
        w2slot_free = {s: [] for s in range(8)}
        for e_ in range(E):
            rr = e_ % 2
            li = self.dma("act", idxf[:, rr],
                          d["fin_h"][e_ * FINW:e_ * FINW + 256]
                          .rearrange("(p u) -> p u", p=128), sem="cif",
                          reads=comp_done + idx_free[rr])
            ii1 = self.step("dve", lambda e, rr=rr: e.tensor_copy(
                idxi[:, rr], idxf[:, rr]), reads=li)
            ii2 = self.step("dve", lambda e, rr=rr, e_=e_: e.tensor_scalar_add(
                idxg[:, rr], idxi[:, rr], e_ * T), reads=ii1)
            gg = []
            for mi in range(2):
                gg = self.emit("pool", lambda e, rr=rr, mi=mi: e.indirect_dma_start(
                    gts[:, rr, mi:mi + 1], None, aff_flat,
                    IOA(ap=idxg[:, rr, mi:mi + 1], axis=0)),
                    reads=ii2 + sta + (gg_prev if mi == 0 else gg),
                    incs=[("cgg", 16)])
            gg_prev = gg
            gx = []
            for mi in range(2):
                gx = self.emit("pool", lambda e, rr=rr, mi=mi: e.indirect_dma_start(
                    xg32[:, rr, mi, :], None, d["xln2_h"][:],
                    IOA(ap=idxi[:, rr, mi:mi + 1], axis=0)),
                    reads=(ii1 + [self.tok1("dxl2")] + xg_free[rr] + gx_prev)
                    if mi == 0 else gx,
                    incs=[("cgx", 16)])
            gx_prev = gx
            cv = []
            for mi in range(2):
                cv += self.step("dve", lambda e, rr=rr, mi=mi: e.tensor_copy(
                    xbf[:, mi, :], xg32[:, rr, mi, :]),
                    reads=gx + (xe_free[rr] if mi == 0 else []) + [self.tok1("dve")])
            tr_ev = []
            for mi in range(2):
                for q in range(2):
                    mmt = []
                    for k in range(4):
                        dc = q * 4 + k
                        mmt = self.emit("pe", lambda e, mi=mi, dc=dc, k=k: e.transpose(
                            pHb[:, k * 128:(k + 1) * 128],
                            xbf[:, mi, dc * 128:(dc + 1) * 128], self.id128b[:]),
                            reads=(cv + bank.get("pH", []) if k == 0 else []),
                            incs=[("pe", 1)] if k == 3 else [])
                    ev = self.step("act", lambda e, mi=mi, q=q, rr=rr: e.copy(
                        xe[:, rr, q * 4:(q + 1) * 4, mi * 128:(mi + 1) * 128],
                        pHb[:, 0:512].rearrange("p (a b) -> p a b", b=128)),
                        reads=mmt)
                    bank["pH"] = ev
                    tr_ev += ev
            b1l = self.dma("pool", b1s[:, rr], d["b1"][e_],
                           sem="cb1", reads=b_free[rr] + gx)
            b2l = self.dma("pool", b2r[:, rr],
                           d["b2"][e_:e_ + 1, :].partition_broadcast(128)
                           .rearrange("p a dd -> p (a dd)"),
                           sem="cb2e", reads=b_free[rr] + gx)
            h_toks = []
            for fc in range(32):
                sl = fc % 4
                w1l = self.dma("pool", w1t[:, sl],
                               d["w1"][e_, :, fc * 128:(fc + 1) * 128]
                               .rearrange("(c p) f -> p c f", p=128),
                               sem=f"cw1{sl}", reads=w1slot_free[sl])
                pHx = pA if fc % 2 == 0 else pB_
                mmh = []
                for dc in range(8):
                    mmh = self.emit("pe", lambda e, fc=fc, dc=dc, rr=rr, sl=sl, pHx=pHx:
                                    e.matmul(
                        pHx[:, 0:256],
                        w1t[:, sl, dc, :], xe[:, rr, dc, :],
                        start=(dc == 0), stop=(dc == 7)),
                        reads=(w1l + tr_ev + list(bank.get(("pH", fc % 2), []))
                               if dc == 0 else []),
                        incs=[("pe", 1)] if dc == 7 else [])
                w1slot_free[sl] = mmh
                ev = self.step("act", lambda e, fc=fc, rr=rr, pHx=pHx: e.activation(
                    hT[:, rr, fc, :], pHx[:, 0:256],
                    AF.Gelu_apprx_tanh, bias=b1s[:, rr, fc:fc + 1]),
                    reads=mmh + b1l + (hT_free[rr] if fc == 0 else []))
                bank[("pH", fc % 2)] = ev
                h_toks += ev
            o_mm_last = {}
            for blk in range(4):
                w2l = []
                for fo in range(8):
                    fc = blk * 8 + fo
                    w2l += self.dma("pool", w2t[:, fo, :],
                                    d["w2"][e_, fc * 128:(fc + 1) * 128, :],
                                    sem=f"cw2{fo}", reads=w2slot_free[fo])
                cur = []
                for mi in range(2):
                    pOx = pO1 if mi == 0 else pO2
                    for nh in range(2):
                        mmo = []
                        for fo in range(8):
                            fc = blk * 8 + fo
                            mmo = self.emit("pe",
                                            lambda e, fc=fc, fo=fo, mi=mi, nh=nh,
                                            rr=rr, pOx=pOx: e.matmul(
                                                pOx[:, nh * 512:(nh + 1) * 512],
                                                hT[:, rr, fc, mi * 128:(mi + 1) * 128],
                                                w2t[:, fo, nh * 512:(nh + 1) * 512],
                                                start=(fc == 0), stop=(fc == 31)),
                                            reads=((h_toks if blk == 0 and fo == 0 else []) +
                                                   (w2l if fo == 0 else []) +
                                                   (list(bank.get(("pO", mi, nh), []))
                                                    if blk == 0 and fo == 0 else []) +
                                                   (bank.get("pO1", []) + bank.get("pO2", [])
                                                    if e_ == 0 and blk == 0 and fo == 0 else [])),
                                            incs=[("pe", 1)] if fo == 7 else [])
                        cur += mmo
                        o_mm_last[(mi, nh)] = mmo
                for fo in range(8):
                    w2slot_free[fo] = cur
            bank["pO1"] = []
            bank["pO2"] = []
            o_toks = []
            for mi in range(2):
                pOx = pO1 if mi == 0 else pO2
                for nh in range(2):
                    t1 = self.step("dve", lambda e, mi=mi, nh=nh, rr=rr, pOx=pOx:
                                   e.tensor_tensor(
                                       osb2[:, rr, mi, nh * 512:(nh + 1) * 512],
                                       pOx[:, nh * 512:(nh + 1) * 512],
                                       b2r[:, rr, nh * 512:(nh + 1) * 512], ALU.add),
                                   reads=o_mm_last[(mi, nh)] + b2l +
                                         (os_free[rr] if mi == 0 and nh == 0 else []))
                    t2 = self.step("dve", lambda e, mi=mi, nh=nh, rr=rr:
                                   e.tensor_scalar_mul(
                                       osb2[:, rr, mi, nh * 512:(nh + 1) * 512],
                                       osb2[:, rr, mi, nh * 512:(nh + 1) * 512],
                                       gts[:, rr, mi:mi + 1]), reads=t1 + gg)
                    o_toks += t2
                    bank[("pO", mi, nh)] = t2
            b_free[rr] = o_toks[-1:] + h_toks[-1:]
            xe_free[rr] = h_toks[-1:]
            hT_free[rr] = o_toks[-1:]
            xg_free[rr] = cv
            sc = []
            for mi in range(2):
                sc = self.emit("pool", lambda e, rr=rr, mi=mi: e.indirect_dma_start(
                    d["out"][:], IOA(ap=idxi[:, rr, mi:mi + 1], axis=0),
                    osb2[:, rr, mi, :], None, compute_op=ALU.add),
                    reads=(o_toks + ii1 + sc_prev +
                           ([self.tok1("dout")] if e_ == 0 else []))
                    if mi == 0 else sc,
                    incs=[("csc", 16)])
            sc_prev = sc
            idx_free[rr] = sc
            os_free[rr] = sc
        self.wait("pool", sc_prev + [self.tok1("dout")])
        self.scope.close()
        self.scope = None

    def assemble(self):
        nc = self.nc
        with nc.Block() as block:
            def mk(eng_key):
                def body(e):
                    for item in self.items[eng_key]:
                        if item[0] == "wait":
                            e.wait_ge(self.sems[item[1]], item[2])
                        else:
                            inst = item[1](e)
                            for sem, amt in item[2]:
                                inst = inst.then_inc(self.sems[sem], amt)
                return body
            block.tensor(mk("pe"))
            block.scalar(mk("act"))
            block.vector(mk("dve"))
            block.gpsimd(mk("pool"))
            block.sync(mk("sp"))

    def finish(self):
        self.stack.close()


def host_consts():
    c = {}
    c["id128b"] = np.eye(128, dtype=np.float32)
    c["id128f"] = np.eye(128, dtype=np.float32)
    ss, tt = np.arange(CC)[:, None], np.arange(CC)[None, :]
    mA = np.where(ss <= tt, 0.0, -1e30).astype(np.float32)
    mM = np.where(ss < tt, 0.0, -1e30).astype(np.float32)
    c["maskA"] = np.tile(mA, (1, 8))
    c["maskM"] = np.tile(mM, (1, 8))
    o3 = np.zeros((6, CC), np.float32); o3[0:3] = 1.0
    c["ones3"] = o3
    n3 = np.zeros((6, CC), np.float32); n3[3:6] = -1.0
    c["neg3"] = n3
    bh = np.zeros((128, 2), np.float32)
    bh[0:64, 0] = 1.0; bh[64:128, 1] = 1.0
    c["blkh64"] = bh
    p = np.arange(128)
    c["blk16"] = (p[:, None] // 16 == np.arange(8)[None, :]).astype(np.float32)
    c["exp16"] = (np.arange(8)[:, None] == p[None, :] // 16).astype(np.float32)
    c["comb16"] = (p[:, None] % 16 == np.arange(16)[None, :]).astype(np.float32)
    c["expc16"] = (np.arange(16)[:, None] == p[None, :] % 16).astype(np.float32)
    return c


def host_inputs(inp, b, T):
    import ml_dtypes

    def bf(a):
        return np.ascontiguousarray(a).astype(ml_dtypes.bfloat16)

    m = {}
    m["x"] = np.ascontiguousarray(np.asarray(inp["L0_attn_hidden_states"][b][:T], np.float32))
    m["wqkv"] = np.ascontiguousarray(np.asarray(inp["L0_attn_W_qkv"], np.float32))
    m["wz"] = np.ascontiguousarray(np.asarray(inp["L0_attn_W_z"], np.float32))
    wab = np.zeros((D, 64), np.float32)
    wab[:, 0:16] = np.asarray(inp["L0_attn_W_b"], np.float32)
    wab[:, 32:48] = np.asarray(inp["L0_attn_W_a"], np.float32)
    m["wab"] = wab
    m["dtb"] = np.ascontiguousarray(np.asarray(inp["L0_attn_dt_bias"], np.float32)[:, None])
    m["alog"] = np.ascontiguousarray(np.asarray(inp["L0_attn_A_log"], np.float32)[:, None])
    cw = np.asarray(inp["L0_attn_conv_w"], np.float32)
    cd = np.zeros((KS, 16, 128, 128), np.float32)
    for k in range(KS):
        for fc in range(16):
            np.fill_diagonal(cd[k, fc], cw[fc * 128:(fc + 1) * 128, k])
    m["convdiag"] = cd
    m["normw"] = np.ascontiguousarray(np.asarray(inp["L0_attn_norm_w"], np.float32)[None, :])
    m["wout"] = np.ascontiguousarray(np.asarray(inp["L0_attn_W_out"], np.float32))
    m["gqt"] = np.ascontiguousarray(np.asarray(inp["L0_ffn_gate_query"], np.float32).T)
    m["ltemp"] = np.ascontiguousarray(np.asarray(inp["L0_ffn_log_temp"], np.float32).reshape(1, 1))
    m["w1"] = np.ascontiguousarray(np.asarray(inp["L0_ffn_w1"], np.float32))
    m["b1"] = np.ascontiguousarray(np.asarray(inp["L0_ffn_b1"], np.float32)
                                   .reshape(E, DFF // 128, 128).transpose(0, 2, 1))
    m["w2"] = np.ascontiguousarray(np.asarray(inp["L0_ffn_w2"], np.float32))
    m["b2"] = np.ascontiguousarray(np.asarray(inp["L0_ffn_b2"], np.float32))
    m["cvals"] = np.array([[LNEPS, 1.0, EPS, 0.5]], np.float32)
    NT8 = T // 8
    TILE = min(128, NT8)
    NT = T // TILE
    NW = min((512 + TILE - 1) // TILE, NT)
    PC = (16 * NT + 127) // 128
    FINW = 384
    p = np.arange(128)
    f = np.arange(NT8)
    n_tok = (p[:, None] // 16) * NT8 + f[None, :]
    m["zarr"] = (T - n_tok).astype(np.int32)
    m["iota32p"] = np.tile(np.arange(1, 33, dtype=np.float32)[None, :], (TILE, 1))
    m["iotawp"] = np.tile(np.arange(1, TILE + 1, dtype=np.float32)[None, :], (TILE, 1))
    pp = np.arange(TILE)
    m["nhl"] = bf(np.stack([pp // 64, pp % 64 + 1], 1).astype(np.float32))
    m["mstrict"] = ((p[:, None] % 16 == p[None, :] % 16) &
                    (p[:, None] // 16 < p[None, :] // 16)).astype(np.float32)
    m["tc32"] = np.tile((np.arange(NT) * TILE - 1).astype(np.float32)[None, :], (32, 1))
    m["tcw"] = np.tile((np.arange(NW) * TILE - 1).astype(np.float32)[None, :], (TILE, 1))
    m["econ2"] = (np.arange(16) * FINW).astype(np.float32)[:, None]
    for k, v in host_consts().items():
        if k in ("id128b", "maskA", "maskM", "ones3", "neg3", "blkh64"):
            m[k] = bf(v)
        else:
            m[k] = v.astype(np.float32)
    return m



def kernel(**inputs):
    from concourse.bass_utils import run_bass_kernel_spmd
    T = 8192
    k = KB(T)
    k.declare_common()
    k.pass_a()
    k.pass_b()
    k.pass_c()
    k.assemble()
    k.finish()
    mybir.codegen_inst_isa_subclasses(k.nc)
    in_maps = [host_inputs(inputs, b, T) for b in range(8)]
    res = run_bass_kernel_spmd(k.nc, in_maps, core_ids=list(range(8)))
    return np.stack(
        [np.asarray(res.results[i]["out"], np.float32) for i in range(8)], axis=0)
